# version 07 (final)
# speedup vs baseline: 1.4088x; 1.0159x over previous
import contextlib
import numpy as np
import ml_dtypes
import concourse.bass as bass
import concourse.mybir as mybir
from concourse.bass_utils import run_bass_kernel_spmd

F32 = mybir.dt.float32
BF16 = mybir.dt.bfloat16
F16 = mybir.dt.float16
U8 = mybir.dt.uint8
AF = mybir.ActivationFunctionType
ALU = mybir.AluOpType
AX = mybir.AxisListType.X

MAGIC = 12582912.0
D = 1024
TC = 2048
H, E = 16, 64
NSB, SBL = 4, 512
I0, I1 = 2816, 3072
NT0, NT1 = 22, 24
EPS = 1e-5

WB_L0G = 0
WB_L0U = 3 * 1024
WB_L0D = 6 * 1024
WB_L1G = WB_L0D + I0
WB_L1U = WB_L1G + 3 * 1024
WB_L1D = WB_L1U + 3 * 1024
WB_ROWS = WB_L1D + I1


class Ctr:
    def __init__(self, sem):
        self.sem = sem
        self.v = 0

    def inc(self, n):
        self.v += n
        return self.v


def build(stage=9, dbg=(), race=True, p1=9, serial_down=False, serial_asm=False):
    nc = bass.Bass("TRN2", target_bir_lowering=False, detect_race_conditions=race)

    def din(name, shape, dt=F32):
        return nc.dram_tensor(name, shape, dt, kind="ExternalInput").ap()

    def dout(name, shape, dt=F32):
        return nc.dram_tensor(name, shape, dt, kind="ExternalOutput").ap()

    def dint(name, shape, dt=F32):
        return nc.dram_tensor(name, shape, dt).ap()

    x_in = din("x_fm", [D, TC])
    qw_t = din("qw_t", [D, D]); kw_t = din("kw_t", [D, D])
    vw_t = din("vw_t", [D, D]); ow_t = din("ow_t", [D, D])
    cs1_t = din("cs1_t", [D, 256]); cs2_t = din("cs2_t", [256, 16])
    cs_b1 = din("cs_b1", [256, 1]); cs_b2 = din("cs_b2", [16, 1])
    decay_he = din("decay_he", [128, 8])
    clip_x = din("clip_x", [1, 1]); clip_down = din("clip_down", [1, 1])
    l0g_own = din("l0g_own", [3 * 128, D]); l0u_own = din("l0u_own", [3 * 128, D])
    l0d_own = din("l0d_own", [128, I0])
    l1g_own = din("l1g_own", [3 * 128, D]); l1u_own = din("l1u_own", [3 * 128, D])
    l1d_own = din("l1d_own", [128, I1])
    Wq_t = din("Wq_t", [D, D]); Wk_t = din("Wk_t", [D, D])
    Wv_t = din("Wv_t", [D, D]); Wo_t = din("Wo_t", [D, D])
    bq_c = din("bq", [D, 1]); bk_c = din("bk", [D, 1]); bo_c = din("bo", [D, 1])
    bv_row = din("bv_row", [1, D])
    ident_in = din("ident_f", [128, 128])
    selmask_in = din("selmask", [128, 512], U8)
    mask32_in = din("mask32", [128, SBL])
    sflags_in = din("sflags_bc", [128, 4])

    out_fm = dout("out_fm", [D, TC])

    wb_in = dint("wb_in", [WB_ROWS, 128], BF16)
    wb_g = dint("wb_g", [8 * WB_ROWS, 128], BF16)
    st_in = dint("st_in", [D, E + 1]); st_g = dint("st_g", [4 * D, E + 1])
    AR_IN = [dint(f"arin{i}", [1, 2]) for i in range(6)]
    AR_OUT = [dint(f"arout{i}", [1, 2]) for i in range(6)]
    kvk_in = dint("kvk_in", [D, TC], BF16); kvk_g = dint("kvk_g", [4 * D, TC], BF16)
    kvv_in = dint("kvv_in", [TC, D], BF16); kvv_g = dint("kvv_g", [4 * TC, D], BF16)
    attn0_st = dint("attn0_st", [D, TC])
    x1_st = dint("x1_st", [D, TC])
    g_dram = dint("g_dram", [16, TC])
    stats_dram = dint("stats_dram", [1, 4 * TC])
    graw_st = dint("graw_st", [I1, TC], F16)
    uraw_st = dint("uraw_st", [I1, TC], F16)
    g2_st = dint("g2_st", [I1, TC], F16)
    oo_st = dint("oo_st", [I1, TC], F16)
    dwn0 = dint("dwn0", [NT0 * 128, 1024], BF16)
    dwn1 = dint("dwn1", [NT1 * 128, 1024], BF16)

    DBG = {}
    def mkdbg(name, shape, dt=F32):
        if name in dbg:
            DBG[name] = dout("dbg_" + name, shape, dt)
        return DBG.get(name)

    ctx = contextlib.ExitStack()
    ds = Ctr(ctx.enter_context(nc.semaphore("ds")))
    dg = Ctr(ctx.enter_context(nc.semaphore("dg")))
    sv = Ctr(ctx.enter_context(nc.semaphore("sv")))
    ss = Ctr(ctx.enter_context(nc.semaphore("ss")))
    st = Ctr(ctx.enter_context(nc.semaphore("st")))
    sg = Ctr(ctx.enter_context(nc.semaphore("sg")))
    da = Ctr(ctx.enter_context(nc.semaphore("da")))
    dx = Ctr(ctx.enter_context(nc.semaphore("dx")))
    dw = Ctr(ctx.enter_context(nc.semaphore("dw")))
    dy = Ctr(ctx.enter_context(nc.semaphore("dy")))
    cc = Ctr(ctx.enter_context(nc.semaphore("cc")))

    P = [ctx.enter_context(nc.psum_tensor(f"pb{i}", [128, 512], F32)) for i in range(8)]

    sb_small = ctx.enter_context(nc.sbuf_tensor("sb_small", [128, 128], F32))
    sb_srow = ctx.enter_context(nc.sbuf_tensor("sb_srow", [1, 192], F32))
    sb_csum = ctx.enter_context(nc.sbuf_tensor("sb_csum", [128, 256], F32))
    sb_expSC = ctx.enter_context(nc.sbuf_tensor("sb_expSC", [128, 512], F32))
    sb_state = ctx.enter_context(nc.sbuf_tensor("sb_state", [128, 8 * E], F32))
    sb_sin = ctx.enter_context(nc.sbuf_tensor("sb_sin", [128, 8 * E], F32))
    sb_dg = ctx.enter_context(nc.sbuf_tensor("sb_dgv", [128, 256], F32))
    sb_ml = ctx.enter_context(nc.sbuf_tensor("sb_ml", [128, 64], F32))
    sb_idf = ctx.enter_context(nc.sbuf_tensor("sb_idf", [128, 128], F32))
    sb_selm = ctx.enter_context(nc.sbuf_tensor("sb_selm", [128, 512], U8))
    sb_m32 = ctx.enter_context(nc.sbuf_tensor("sb_m32", [128, SBL], F32))
    sb_zero = ctx.enter_context(nc.sbuf_tensor("sb_zero", [128, 512], F32))

    C_ONES = 0
    C_DECAY = 8
    C_FLAGS = 16
    C_CLIPX = 20
    C_CLIPD = 21
    C_CSB1 = 24
    C_CSB2 = 26
    C_BASE = 32
    C_OMF = 40
    SLOT = {}
    def slot(name):
        if name not in SLOT:
            SLOT[name] = 48 + len(SLOT)
            assert SLOT[name] < 128
        return sb_small[:, SLOT[name]:SLOT[name] + 1]

    def fap(t, off, dims):
        return bass.AP(t, off, dims)

    ones_col = sb_small[:, C_ONES:C_ONES + 1]

    _fenced = [None]
    def fence():
        cur = (ds.v, dg.v, da.v, dx.v, dw.v, dy.v)
        if cur == _fenced[0]:
            return
        _fenced[0] = cur
        with nc.Block() as b:
            @b.sync
            def _(e):
                for ctr in (ds, dg, da, dx, dw, dy):
                    if ctr.v:
                        e.wait_ge(ctr.sem, ctr.v)
    LIVE = []
    def arena():
        a = nc.auto_arena()
        orig_alloc, orig_free = a.allocate, a.free
        def alloc(*ar, **kw):
            orig_alloc(*ar, **kw)
            LIVE.append(a)
        def free():
            assert LIVE and LIVE[-1] is a, "non-stack arena free"
            fence()
            LIVE.pop()
            orig_free()
        a.allocate, a.free = alloc, free
        return a

    A_xn = arena()
    sb_xn = A_xn.sbuf_tensor("sb_xn", [128, 8 * TC], BF16)
    A_xn.allocate()

    A_W = arena()
    sb_w1 = A_W.sbuf_tensor("sb_w1", [128, I1], F32)
    sb_w2 = A_W.sbuf_tensor("sb_w2", [128, I1], F32)
    sb_wb = A_W.sbuf_tensor("sb_wb", [128, I1], BF16)
    sb_bsc = A_W.sbuf_tensor("sb_bsc", [128, 96], F32)
    sb_brs = A_W.sbuf_tensor("sb_brs", [128, 96], F32)
    sb_cs1 = A_W.sbuf_tensor("sb_cs1", [128, 8 * 256], BF16)
    sb_cs2 = A_W.sbuf_tensor("sb_cs2", [128, 32], BF16)
    sb_g = A_W.sbuf_tensor("sb_gv", [16, TC], F32)
    sb_h1 = A_W.sbuf_tensor("sb_h1", [128, 2 * TC], BF16)
    A_W.allocate()


    with nc.Block() as blk:
        @blk.sync
        def _(e):
            e.dma_start(out=sb_idf[:, :], in_=ident_in[:, :]).then_inc(ds.sem, 16)
            e.dma_start(out=sb_selm[:, :], in_=selmask_in[:, :]).then_inc(ds.sem, 16)
            e.dma_start(out=sb_m32[:, :], in_=mask32_in[:, :]).then_inc(ds.sem, 16)
            e.dma_start(out=sb_small[:, C_DECAY:C_DECAY + 8], in_=decay_he[:, :]).then_inc(ds.sem, 16)
            e.dma_start(out=sb_small[:, C_FLAGS:C_FLAGS + 4], in_=sflags_in[:, :]).then_inc(ds.sem, 16)
            e.dma_start(out=sb_small[0:1, C_CLIPX:C_CLIPX + 1], in_=clip_x[:, :]).then_inc(ds.sem, 16)
            e.dma_start(out=sb_small[0:1, C_CLIPD:C_CLIPD + 1], in_=clip_down[:, :]).then_inc(ds.sem, 16)
            e.dma_start(out=sb_small[0:128, C_CSB1:C_CSB1 + 2],
                        in_=fap(cs_b1.tensor, 0, [[1, 128], [128, 2]]),
                        allow_slow_non_contiguous=True).then_inc(ds.sem, 16)
            e.dma_start(out=sb_small[0:16, C_CSB2:C_CSB2 + 1], in_=cs_b2[:, :]).then_inc(ds.sem, 16)
            e.wait_ge(ds.sem, ds.inc(9 * 16))
        @blk.gpsimd
        def _(e):
            e.memset(sb_zero[:, :], 0.0)
            e.memset(sb_small[:, C_ONES:C_ONES + 1], 1.0)
            e.memset(sb_state[:, :], 0.0)
            e.memset(sb_sin[:, :], 0.0)
            e.dma_start(out=sb_cs1[:, :],
                        in_=fap(cs1_t.tensor, 0, [[256, 128], [128 * 256, 8], [1, 256]])).then_inc(dg.sem, 16)
            e.dma_start(out=sb_cs2[:, 0:16], in_=cs2_t[0:128, :]).then_inc(dg.sem, 16)
            e.dma_start(out=sb_cs2[:, 16:32], in_=cs2_t[128:256, :]).then_inc(dg.sem, 16)
            dg.inc(3 * 16)
    with nc.Block() as blk:
        @blk.scalar
        def _(e):
            e.activation(sb_small[:, C_BASE:C_BASE + 8], sb_small[:, C_DECAY:C_DECAY + 8], AF.Sigmoid)
        @blk.vector
        def _(e):
            e.tensor_scalar(sb_small[:, C_OMF:C_OMF + 4], sb_small[:, C_FLAGS:C_FLAGS + 4],
                            -1.0, 1.0, ALU.mult, ALU.add)

    def wb_store(ncols, dst_row0):
        with nc.Block() as b:
            @b.sync
            def _(e):
                e.dma_start(
                    out=fap(wb_in.tensor, dst_row0 * 128, [[ncols, 128], [1, ncols]]),
                    in_=fap(sb_wb, 0, [[I1, 128], [1, ncols]]),
                ).then_inc(ds.sem, 16)
                e.wait_ge(ds.sem, ds.inc(16))

    def transpose_body(ncols):
        nb = ncols // 128
        for g0 in range(0, nb, 4):
            gn = min(4, nb - g0)
            with nc.Block() as b:
                @b.tensor
                def _(e, g0=g0, gn=gn):
                    for j in range(gn):
                        e.transpose(P[j][0:128, 0:128],
                                    sb_w2[:, (g0 + j) * 128:(g0 + j + 1) * 128],
                                    sb_idf[:, :])
            with nc.Block() as b:
                @b.vector
                def _(e, g0=g0, gn=gn):
                    for j in range(gn):
                        e.tensor_copy(sb_wb[:, (g0 + j) * 128:(g0 + j + 1) * 128],
                                      P[j][0:128, 0:128])

    def w8_tile(own_ap, ti, ncols, dst_row0):
        with nc.Block() as b:
            @b.sync
            def _(e):
                e.dma_start(out=sb_w1[:, 0:ncols],
                            in_=own_ap[ti * 128:(ti + 1) * 128, :]).then_inc(ds.sem, 16)
                e.wait_ge(ds.sem, ds.inc(16))
        with nc.Block() as b:
            @b.vector
            def _(e):
                rmax, rs, sc = slot("w_rmax"), slot("w_rs"), slot("w_sc")
                e.reduce_max(rmax, sb_w1[:, 0:ncols], AX, apply_absolute_value=True)
                e.drain()
                e.reciprocal(rs, rmax)
                e.drain()
                e.tensor_scalar(rs, rs, 127.0, None, ALU.mult)
                e.tensor_scalar(sc, rmax, 1.0 / 127.0, None, ALU.mult)
                e.drain()
                e.tensor_scalar(sb_w2[:, 0:ncols], sb_w1[:, 0:ncols], rs, MAGIC, ALU.mult, ALU.add)
                e.drain()
                e.tensor_scalar(sb_w2[:, 0:ncols], sb_w2[:, 0:ncols], -MAGIC, None, ALU.add)
                e.drain()
                e.tensor_scalar(sb_w2[:, 0:ncols], sb_w2[:, 0:ncols], sc, None, ALU.mult)
                e.drain()
        transpose_body(ncols)
        wb_store(ncols, dst_row0)

    def lpbq_tile(own_ap, ti, ncols, dst_row0):
        nblk = ncols // 32
        with nc.Block() as b:
            @b.sync
            def _(e):
                e.dma_start(out=sb_w1[:, 0:ncols],
                            in_=own_ap[ti * 128:(ti + 1) * 128, :]).then_inc(ds.sem, 16)
                e.wait_ge(ds.sem, ds.inc(16))
        with nc.Block() as b:
            @b.vector
            def _(e):
                bm = sb_bsc[:, 0:nblk]
                rs = sb_brs[:, 0:nblk]
                e.tensor_reduce(bm, fap(sb_w1, 0, [[I1, 128], [32, nblk], [1, 32]]),
                                AX, ALU.max, apply_absolute_value=True)
                e.drain()
                e.tensor_scalar(bm, bm, 1.0 / 7.0, 1e-12, ALU.mult, ALU.max)
                e.drain()
                e.reciprocal(rs, bm)
                e.drain()
                e.tensor_tensor(fap(sb_w2, 0, [[I1, 128], [32, nblk], [1, 32]]),
                                fap(sb_w1, 0, [[I1, 128], [32, nblk], [1, 32]]),
                                fap(sb_brs, 0, [[96, 128], [1, nblk], [0, 32]]), ALU.mult)
                e.drain()
                e.tensor_scalar(sb_w2[:, 0:ncols], sb_w2[:, 0:ncols], MAGIC, -MAGIC, ALU.add, ALU.add)
                e.drain()
                e.tensor_tensor(fap(sb_w2, 0, [[I1, 128], [32, nblk], [1, 32]]),
                                fap(sb_w2, 0, [[I1, 128], [32, nblk], [1, 32]]),
                                fap(sb_bsc, 0, [[96, 128], [1, nblk], [0, 32]]), ALU.mult)
                e.drain()
        transpose_body(ncols)
        wb_store(ncols, dst_row0)

    for t in range(3):
        w8_tile(l0g_own, t, D, WB_L0G + t * 1024)
        w8_tile(l0u_own, t, D, WB_L0U + t * 1024)
    w8_tile(l0d_own, 0, I0, WB_L0D)
    for t in range(3):
        lpbq_tile(l1g_own, t, D, WB_L1G + t * 1024)
        lpbq_tile(l1u_own, t, D, WB_L1U + t * 1024)
    lpbq_tile(l1d_own, 0, I1, WB_L1D)

    with nc.Block() as b:
        @b.gpsimd
        def _(e):
            e.collective_compute("AllGather", ALU.bypass,
                                 replica_groups=[list(range(8))],
                                 ins=[wb_in[:, :]], outs=[wb_g[:, :]]).then_inc(cc.sem, 1)
            cc.inc(1)
    CC_W = cc.v

    def layer_norm(src, xn_dst, ar_idx, tag):
        A_LN = arena()
        sb_xt = A_LN.sbuf_tensor("sb_xt", [128, TC], F32)
        sb_sq = A_LN.sbuf_tensor("sb_sq", [128, TC], F32)
        sb_binv = A_LN.sbuf_tensor("sb_binv", [128, TC], F32)
        sb_bnmi = A_LN.sbuf_tensor("sb_bnmi", [128, TC], F32)
        sb_st4 = A_LN.sbuf_tensor("sb_st4", [1, 4 * TC], F32)
        A_LN.allocate()
        for k in range(8):
            with nc.Block() as b:
                @b.sync
                def _(e, k=k):
                    e.dma_start(out=sb_xt[:, :], in_=src[k * 128:(k + 1) * 128, :]).then_inc(ds.sem, 16)
                    e.wait_ge(ds.sem, ds.inc(16))
            with nc.Block() as b:
                @b.scalar
                def _(e, k=k):
                    e.activation(sb_sq[:, :], sb_xt[:, :], AF.Square)
            with nc.Block() as b:
                @b.tensor
                def _(e, k=k):
                    for tg in range(4):
                        e.matmul(P[tg][0:1, :], ones_col, sb_xt[:, tg * 512:(tg + 1) * 512],
                                 start=(k == 0), stop=(k == 7))
                        e.matmul(P[4 + tg][0:1, :], ones_col, sb_sq[:, tg * 512:(tg + 1) * 512],
                                 start=(k == 0), stop=(k == 7))
        r0 = fap(sb_st4, 0 * TC, [[4 * TC, 1], [1, TC]])
        r1 = fap(sb_st4, 1 * TC, [[4 * TC, 1], [1, TC]])
        r2 = fap(sb_st4, 2 * TC, [[4 * TC, 1], [1, TC]])
        r3 = fap(sb_st4, 3 * TC, [[4 * TC, 1], [1, TC]])
        with nc.Block() as b:
            @b.vector
            def _(e):
                for tg in range(4):
                    e.tensor_copy(fap(sb_st4, tg * 512, [[4 * TC, 1], [1, 512]]), P[tg][0:1, :])
                    e.tensor_copy(fap(sb_st4, TC + tg * 512, [[4 * TC, 1], [1, 512]]), P[4 + tg][0:1, :])
        with nc.Block() as b:
            @b.vector
            def _(e):
                e.tensor_scalar(r0, r0, 1.0 / D, None, ALU.mult)
                e.tensor_scalar(r1, r1, 1.0 / D, None, ALU.mult)
                e.drain()
                e.tensor_tensor(r2, r0, r0, ALU.mult)
                e.drain()
                e.tensor_tensor(r3, r1, r2, ALU.subtract)
                e.drain()
                e.tensor_scalar(r3, r3, EPS, None, ALU.add)
        with nc.Block() as b:
            @b.scalar
            def _(e):
                e.activation(r1, r3, AF.Ln)
                e.activation(r2, r1, AF.Exp, scale=-0.5)
        with nc.Block() as b:
            @b.vector
            def _(e):
                e.scalar_tensor_tensor(r3, r0, -1.0, r2, ALU.mult, ALU.mult)
        with nc.Block() as b:
            @b.sync
            def _(e):
                e.dma_start(out=stats_dram[0:1, 2 * TC:4 * TC],
                            in_=fap(sb_st4, 2 * TC, [[4 * TC, 1], [1, 2 * TC]])).then_inc(ds.sem, 16)
                e.wait_ge(ds.sem, ds.inc(16))
        with nc.Block() as b:
            @b.sync
            def _(e):
                e.dma_start(out=sb_binv[:, :],
                            in_=fap(stats_dram.tensor, 2 * TC, [[0, 128], [1, TC]])).then_inc(ds.sem, 16)
                e.dma_start(out=sb_bnmi[:, :],
                            in_=fap(stats_dram.tensor, 3 * TC, [[0, 128], [1, TC]])).then_inc(ds.sem, 16)
                e.wait_ge(ds.sem, ds.inc(32))
        amax = slot(tag + "_amax")
        tmpm = slot(tag + "_tmpm")
        for k in range(8):
            with nc.Block() as b:
                @b.sync
                def _(e, k=k):
                    e.dma_start(out=sb_xt[:, :], in_=src[k * 128:(k + 1) * 128, :]).then_inc(ds.sem, 16)
                    e.wait_ge(ds.sem, ds.inc(16))
            with nc.Block() as b:
                @b.vector
                def _(e, k=k):
                    e.tensor_tensor(sb_sq[:, :], sb_xt[:, :], sb_binv[:, :], ALU.mult)
                    e.tensor_tensor(xn_dst[:, k * TC:(k + 1) * TC], sb_sq[:, :], sb_bnmi[:, :], ALU.add)
                    e.reduce_max(tmpm, xn_dst[:, k * TC:(k + 1) * TC], AX, apply_absolute_value=True)
                    e.drain()
                    if k == 0:
                        e.tensor_copy(amax, tmpm)
                    else:
                        e.tensor_tensor(amax, amax, tmpm, ALU.max)
                    e.drain()
        with nc.Block() as b:
            @b.sync
            def _(e):
                e.dma_start(out=fap(sb_srow, 0, [[192, 1], [1, 128]]),
                            in_=fap(sb_small, SLOT[tag + "_amax"], [[128, 128], [1, 1]])).then_inc(ds.sem, 16)
                e.wait_ge(ds.sem, ds.inc(16))
        with nc.Block() as b:
            @b.vector
            def _(e):
                e.reduce_max(fap(sb_srow, 128, [[192, 1], [1, 1]]), fap(sb_srow, 0, [[192, 1], [1, 128]]), AX)
                e.drain()
                e.tensor_copy(fap(sb_srow, 129, [[192, 1], [1, 1]]), fap(sb_srow, 128, [[192, 1], [1, 1]]))
        with nc.Block() as b:
            @b.sync
            def _(e):
                e.dma_start(out=AR_IN[ar_idx][:, :], in_=fap(sb_srow, 128, [[192, 1], [1, 2]])).then_inc(ds.sem, 16)
                e.wait_ge(ds.sem, ds.inc(16))
        with nc.Block() as b:
            @b.gpsimd
            def _(e):
                e.collective_compute("AllReduce", ALU.max,
                                     replica_groups=[list(range(8))],
                                     ins=[AR_IN[ar_idx][:, :]],
                                     outs=[AR_OUT[ar_idx][:, :]]).then_inc(cc.sem, 1)
                cc.inc(1)
        A_LN.free()
        return cc.v

    CC_AR0 = layer_norm(x_in, sb_xn, 0, "ln0")

    def dump_fm(name, src_sb, wdt=BF16, ncol=TC, ntile=8):
        t = mkdbg(name, [ntile * 128, ncol], wdt)
        if t is None:
            return
        with nc.Block() as b:
            @b.sync
            def _(e):
                for k in range(ntile):
                    e.dma_start(out=t[k * 128:(k + 1) * 128, :],
                                in_=src_sb[:, k * ncol:(k + 1) * ncol]).then_inc(ds.sem, 16)
                e.wait_ge(ds.sem, ds.inc(ntile * 16))

    dump_fm("xn0", sb_xn)

    with nc.Block() as b:
        @b.sync
        def _(e):
            e.wait_ge(dg.sem, dg.v)
    for dh in range(2):
        with nc.Block() as b:
            @b.tensor
            def _(e, dh=dh):
                for tg in range(4):
                    for k in range(8):
                        e.matmul(P[tg][:, :],
                                 sb_cs1[:, k * 256 + dh * 128:k * 256 + (dh + 1) * 128],
                                 sb_xn[:, k * TC + tg * 512:k * TC + (tg + 1) * 512],
                                 start=(k == 0), stop=(k == 7))
        with nc.Block() as b:
            @b.scalar
            def _(e, dh=dh):
                for tg in range(4):
                    e.activation(sb_h1[:, dh * TC + tg * 512:dh * TC + (tg + 1) * 512],
                                 P[tg][:, :], AF.Tanh,
                                 bias=sb_small[:, C_CSB1 + dh:C_CSB1 + dh + 1])
    with nc.Block() as b:
        @b.tensor
        def _(e):
            for tg in range(4):
                for dh in range(2):
                    e.matmul(P[4 + tg][0:16, :],
                             sb_cs2[:, dh * 16:(dh + 1) * 16],
                             sb_h1[:, dh * TC + tg * 512:dh * TC + (tg + 1) * 512],
                             start=(dh == 0), stop=(dh == 1))
    with nc.Block() as b:
        @b.scalar
        def _(e):
            for tg in range(4):
                e.activation(sb_g[0:16, tg * 512:(tg + 1) * 512], P[4 + tg][0:16, :],
                             AF.Sigmoid, bias=sb_small[0:16, C_CSB2:C_CSB2 + 1])
    with nc.Block() as b:
        @b.vector
        def _(e):
            e.tensor_scalar(sb_g[0:16, :], sb_g[0:16, :], 0.2, 1.0, ALU.mult, ALU.add)
    with nc.Block() as b:
        @b.sync
        def _(e):
            e.dma_start(out=g_dram[:, :], in_=sb_g[0:16, :]).then_inc(ds.sem, 16)
            e.wait_ge(ds.sem, ds.inc(16))
    if "gv" in dbg:
        t = mkdbg("gv", [16, TC])
        with nc.Block() as b:
            @b.sync
            def _(e):
                e.dma_start(out=t[:, :], in_=sb_g[0:16, :]).then_inc(ds.sem, 16)
                e.wait_ge(ds.sem, ds.inc(16))

    A_W.free()

    if stage >= 1:
        A_y0 = arena()
        y0 = A_y0.sbuf_tensor("y0", [128, 16 * 1024], BF16)
        A_y0.allocate()
        A_B1a = arena()
        qh = A_B1a.sbuf_tensor("qh", [128, 8 * TC], BF16)
        scA = A_B1a.sbuf_tensor("scA", [128, 512], F32)
        sq1a = A_B1a.sbuf_tensor("sq1a", [128, 512], BF16)
        sq1b = A_B1a.sbuf_tensor("sq1b", [128, 512], BF16)
        sS2 = A_B1a.sbuf_tensor("sS2", [128, 8 * E], BF16)
        sPc = A_B1a.sbuf_tensor("sPc", [128, 1024], BF16)
        A_B1a.allocate()
        A_B1b = arena()
        Ssb = A_B1b.sbuf_tensor("Ssb", [128, 8 * 512], F32)
        khz = A_B1b.sbuf_tensor("khz", [128, 8 * 1024], BF16)
        Khz = A_B1b.sbuf_tensor("Khz", [128, 8 * 1024], BF16)
        vt = A_B1b.sbuf_tensor("vt", [128, 4 * 1024], BF16)
        wsb = A_B1b.sbuf_tensor("wsb", [128, 2 * 1024], BF16)
        scB = A_B1b.sbuf_tensor("scB", [128, 512], F32)
        sAt = A_B1b.sbuf_tensor("sAt", [128, 1024], BF16)
        sKt = A_B1b.sbuf_tensor("sKt", [128, 1024], BF16)
        sVt = A_B1b.sbuf_tensor("sVt", [128, 1024], BF16)
        sSt = A_B1b.sbuf_tensor("sSt", [128, 8 * E], BF16)
        A_B1b.allocate()
        with nc.Block() as b:
            @b.gpsimd
            def _(e):
                e.memset(sSt[:, :], 0.0)
                e.memset(khz[:, :], 0.0)
                e.memset(Khz[:, :], 0.0)
                e.memset(sq1a[:, :], 0.0)
                e.memset(sq1b[:, :], 0.0)
                e.memset(sAt[:, :], 0.0)
                e.memset(sKt[:, :], 0.0)
                e.memset(sVt[:, :], 0.0)

        for s in range(NSB):
            for p in range(8):
                with nc.Block() as b:
                    @b.sync
                    def _(e, s=s, p=p):
                        e.dma_start(out=scB[0:64, :],
                                    in_=fap(g_dram.tensor, (2 * p) * TC + s * SBL,
                                            [[0, 64], [1, SBL]])).then_inc(ds.sem, 16)
                        e.dma_start(out=scB[64:128, :],
                                    in_=fap(g_dram.tensor, (2 * p + 1) * TC + s * SBL,
                                            [[0, 64], [1, SBL]])).then_inc(ds.sem, 16)
                        e.wait_ge(ds.sem, ds.inc(32))
                with nc.Block() as b:
                    @b.vector
                    def _(e, s=s, p=p):
                        e.tensor_scalar(scB[:, :], scB[:, :],
                                        sb_small[:, C_BASE + p:C_BASE + p + 1], 0.9995,
                                        ALU.mult, ALU.min)
                with nc.Block() as b:
                    @b.scalar
                    def _(e, s=s, p=p):
                        e.activation(scA[:, :], scB[:, :], AF.Ln)
                with nc.Block() as b:
                    @b.vector
                    def _(e, s=s, p=p):
                        e.tensor_tensor_scan(Ssb[:, p * 512:(p + 1) * 512], sb_m32[:, :],
                                             scA[:, :], 0.0, ALU.mult, ALU.add).then_inc(sv.sem, 1)
                        e.wait_ge(sv.sem, sv.inc(1))
                        e.tensor_copy(scA[:, 0:16],
                                      fap(Ssb, p * 512 + 31, [[8 * 512, 128], [32, 16]]))
                        e.drain()
                        e.tensor_tensor(sb_csum[:, 32 * p + 8 * s:32 * p + 8 * s + 8],
                                        fap(scA, 0, [[512, 128], [2, 8]]),
                                        fap(scA, 1, [[512, 128], [2, 8]]), ALU.add)
                with nc.Block() as b:
                    @b.scalar
                    def _(e, s=s, p=p):
                        e.activation(sb_expSC[:, 128 * s + 16 * p:128 * s + 16 * p + 16],
                                     scA[:, 0:16], AF.Exp)
                        e.activation(sb_ml[:, 8 * p:8 * p + 8],
                                     sb_csum[:, 32 * p + 8 * s:32 * p + 8 * s + 8], AF.Exp)

            for wi, w_ap in ((0, qw_t), (1, kw_t)):
                with nc.Block() as b:
                    st_b, dg_b, dy_b = st.v, dg.v, dy.v
                    @b.gpsimd
                    def _(e, wi=wi, w_ap=w_ap, s=s, st_b=st_b):
                        for dk in range(8):
                            if dk >= 2:
                                e.wait_ge(st.sem, st_b + dk - 1)
                            pc = dg if dk % 2 == 0 else dy
                            e.dma_start(out=wsb[:, (dk % 2) * 1024:(dk % 2 + 1) * 1024],
                                        in_=w_ap[dk * 128:(dk + 1) * 128, :]).then_inc(pc.sem, 16)
                            pc.inc(16)
                    @b.tensor
                    def _(e, wi=wi, s=s, dg_b=dg_b, dy_b=dy_b):
                        for dk in range(8):
                            pb, pv = (dg_b, dg) if dk % 2 == 0 else (dy_b, dy)
                            e.wait_ge(pv.sem, pb + (dk // 2 + 1) * 16)
                            for o in range(8):
                                mm = e.matmul(P[o][:, :],
                                         wsb[:, (dk % 2) * 1024 + o * 128:(dk % 2) * 1024 + (o + 1) * 128],
                                         sb_xn[:, dk * TC + s * SBL:dk * TC + (s + 1) * SBL],
                                         start=(dk == 0), stop=(dk == 7))
                            mm.then_inc(st.sem, 1)
                            st.inc(1)
                with nc.Block() as b:
                    ss_b, sv_b = ss.v, sv.v
                    @b.scalar
                    def _(e, wi=wi, s=s, sv_b=sv_b):
                        for p in range(8):
                            if p >= 1:
                                e.wait_ge(sv.sem, sv_b + p)
                            if wi == 0:
                                act = e.activation(scA[:, :], Ssb[:, p * 512:(p + 1) * 512], AF.Exp)
                            else:
                                act = e.activation(scA[:, :], Ssb[:, p * 512:(p + 1) * 512], AF.Exp, scale=-1.0)
                            act.then_inc(ss.sem, 1)
                            ss.inc(1)
                    @b.vector
                    def _(e, wi=wi, s=s, ss_b=ss_b):
                        for p in range(8):
                            e.wait_ge(ss.sem, ss_b + p + 1)
                            if wi == 0:
                                e.tensor_tensor(qh[:, p * TC + s * SBL:p * TC + (s + 1) * SBL],
                                                P[p][:, :], scA[:, :], ALU.mult).then_inc(sv.sem, 1)
                                sv.inc(1)
                            else:
                                e.tensor_tensor(khz[0:64, p * 1024:p * 1024 + 512],
                                                P[p][0:64, :], scA[0:64, :], ALU.mult)
                                e.tensor_tensor(khz[64:128, p * 1024 + 512:(p + 1) * 1024],
                                                P[p][64:128, :], scA[64:128, :], ALU.mult)
                                e.tensor_tensor(scB[:, :], scA[:, :],
                                                fap(sb_expSC, 128 * s + 16 * p,
                                                    [[512, 128], [1, 16], [0, 32]]), ALU.mult)
                                e.drain()
                                e.tensor_tensor(Khz[0:64, p * 1024:p * 1024 + 512],
                                                P[p][0:64, :], scB[0:64, :], ALU.mult)
                                e.tensor_tensor(Khz[64:128, p * 1024 + 512:(p + 1) * 1024],
                                                P[p][64:128, :], scB[64:128, :], ALU.mult).then_inc(sv.sem, 1)
                                sv.inc(1)

            with nc.Block() as b:
                st_b, dg_b, dy_b = st.v, dg.v, dy.v
                @b.gpsimd
                def _(e, s=s, st_b=st_b):
                    for dk in range(8):
                        if dk >= 2:
                            e.wait_ge(st.sem, st_b + dk - 1)
                        pc = dg if dk % 2 == 0 else dy
                        e.dma_start(out=wsb[:, (dk % 2) * 1024:(dk % 2 + 1) * 1024],
                                    in_=vw_t[dk * 128:(dk + 1) * 128, :]).then_inc(pc.sem, 16)
                        pc.inc(16)
                @b.tensor
                def _(e, s=s, dg_b=dg_b, dy_b=dy_b):
                    for dk in range(8):
                        pb, pv = (dg_b, dg) if dk % 2 == 0 else (dy_b, dy)
                        e.wait_ge(pv.sem, pb + (dk // 2 + 1) * 16)
                        for q4 in range(4):
                            for hf in range(2):
                                mm = e.matmul(P[2 * q4 + hf][:, :],
                                         sb_xn[:, dk * TC + s * SBL + q4 * 128:dk * TC + s * SBL + (q4 + 1) * 128],
                                         wsb[:, (dk % 2) * 1024 + hf * 512:(dk % 2) * 1024 + (hf + 1) * 512],
                                         start=(dk == 0), stop=(dk == 7))
                        mm.then_inc(st.sem, 1)
                        st.inc(1)
            with nc.Block() as b:
                @b.vector
                def _(e, s=s):
                    for q4 in range(4):
                        for hf in range(2):
                            e.tensor_copy(vt[:, q4 * 1024 + hf * 512:q4 * 1024 + (hf + 1) * 512],
                                          P[2 * q4 + hf][:, :])

            CL = 3 if p1 >= 2 and p1 < 10 else (p1 - 11 if p1 >= 11 else 0)
            for c in range(8 if CL >= 1 else 0):
                g = s * 8 + c
                tt16 = (s * SBL + c * 64) // 128
                rbase = (64 * c) % 128
                with nc.Block() as b:
                    @b.sync
                    def _(e, s=s, c=c, rbase=rbase):
                        e.dma_start(out=sVt[0:64, :],
                                    in_=vt[rbase:rbase + 64, (c // 2) * 1024:(c // 2 + 1) * 1024]).then_inc(ds.sem, 16)
                        e.wait_ge(ds.sem, ds.inc(16))
                    @b.vector
                    def _(e, s=s, c=c):
                        for p in range(8):
                            col0 = p * 1024 + c * 64
                            col1 = p * 1024 + 512 + c * 64
                            ec = 128 * s + 16 * p + 2 * c
                            e.tensor_scalar(scA[0:64, 64 * p:64 * p + 32], Khz[0:64, col0:col0 + 32],
                                            sb_expSC[0:64, ec + 1:ec + 2], None, ALU.mult)
                            e.tensor_scalar(scA[64:128, 64 * p:64 * p + 32], Khz[64:128, col1:col1 + 32],
                                            sb_expSC[64:128, ec + 1:ec + 2], None, ALU.mult)
                            e.tensor_copy(scA[0:64, 64 * p + 32:64 * p + 64], Khz[0:64, col0 + 32:col0 + 64])
                            e.tensor_copy(scA[64:128, 64 * p + 32:64 * p + 64], Khz[64:128, col1 + 32:col1 + 64])
                            qc0 = p * TC + s * SBL + c * 64
                            e.tensor_copy(sq1a[0:64, 64 * p:64 * p + 32], qh[0:64, qc0:qc0 + 32])
                            e.tensor_scalar(sq1a[0:64, 64 * p + 32:64 * p + 64],
                                            qh[0:64, qc0 + 32:qc0 + 64],
                                            sb_expSC[0:64, ec:ec + 1], None, ALU.mult)
                            e.tensor_copy(sq1b[64:128, 64 * p:64 * p + 32], qh[64:128, qc0:qc0 + 32])
                            e.tensor_scalar(sq1b[64:128, 64 * p + 32:64 * p + 64],
                                            qh[64:128, qc0 + 32:qc0 + 64],
                                            sb_expSC[64:128, ec:ec + 1], None, ALU.mult)
                with nc.Block() as b:
                    @b.tensor
                    def _(e, s=s, c=c):
                        for p in range(8):
                            e.transpose(P[2 + p // 4][0:64, 128 * (p % 4):128 * (p % 4) + 128],
                                        scA[:, 64 * p:64 * (p + 1)], sb_idf[:, :])
                        for p in range(8):
                            col0 = p * 1024 + c * 64
                            col1 = p * 1024 + 512 + c * 64
                            pA = P[p // 4]
                            qcol = p * TC + s * SBL + c * 64
                            for h2 in range(2):
                                kcol = col0 if h2 == 0 else col1
                                fo = 128 * (p % 4) + 64 * h2
                                lhs_k0 = khz[:, kcol:kcol + 32]
                                lhs_k1 = khz[:, kcol + 32:kcol + 64]
                                lhs_K0 = Khz[:, kcol:kcol + 32]
                                rq0 = qh[:, qcol:qcol + 32]
                                rq1 = qh[:, qcol + 32:qcol + 64]
                                e.matmul(pA[0:32, fo:fo + 32], lhs_k0, rq0, start=True, stop=True)
                                e.matmul(pA[32:64, fo + 32:fo + 64], lhs_k1, rq1, start=True, stop=True)
                                e.matmul(pA[0:32, fo + 32:fo + 64], lhs_K0, rq1, start=True, stop=True)
                with nc.Block() as b:
                    @b.vector
                    def _(e, s=s, c=c):
                        for half in range(2):
                            e.select(sAt[0:64, half * 512:(half + 1) * 512],
                                     sb_selm[0:64, :],
                                     P[half][0:64, :], sb_zero[0:64, :])
                            e.tensor_copy(sKt[0:64, half * 512:(half + 1) * 512],
                                          P[2 + half][0:64, :])
                with nc.Block() as b:
                    @b.tensor
                    def _(e, s=s, c=c, rbase=rbase):
                        for p in range(8 if CL >= 2 else 0):
                            pO = P[4 + p // 4]
                            fo = 128 * (p % 4)
                            for h2 in range(2):
                                sq = sq1a if h2 == 0 else sq1b
                                e.matmul(pO[rbase:rbase + 64, fo + 64 * h2:fo + 64 * h2 + 64],
                                         sq[:, 64 * p:64 * (p + 1)],
                                         sSt[:, 64 * p:64 * (p + 1)],
                                         start=True, stop=False)
                                e.matmul(pO[rbase:rbase + 64, fo + 64 * h2:fo + 64 * h2 + 64],
                                         sAt[:, 128 * p + 64 * h2:128 * p + 64 * h2 + 64],
                                         sVt[:, 128 * p + 64 * h2:128 * p + 64 * h2 + 64],
                                         start=False, stop=True)
                        for p in range(8 if CL >= 2 else 0):
                            for h2 in range(2):
                                e.matmul(P[6][64 * h2:64 * h2 + 64, 64 * p:64 * (p + 1)],
                                         sKt[:, 128 * p + 64 * h2:128 * p + 64 * h2 + 64],
                                         sVt[:, 128 * p + 64 * h2:128 * p + 64 * h2 + 64],
                                         start=True, stop=True)
                with nc.Block() as b:
                    @b.vector
                    def _(e, s=s, c=c, g=g, tt16=tt16, rbase=rbase):
                        for half in range(2 if CL >= 3 else 0):
                            e.tensor_copy(y0[rbase:rbase + 64, tt16 * 1024 + half * 512:tt16 * 1024 + (half + 1) * 512],
                                          P[4 + half][rbase:rbase + 64, :])
                        for p in range(8 if CL >= 3 else 0):
                            e.scalar_tensor_tensor(sb_state[:, 64 * p:64 * (p + 1)],
                                                   sb_state[:, 64 * p:64 * (p + 1)],
                                                   sb_ml[:, 8 * p + c:8 * p + c + 1],
                                                   P[6][:, 64 * p:64 * (p + 1)],
                                                   ALU.mult, ALU.add)
                        if CL >= 3:
                            e.drain()
                            e.tensor_copy(sSt[:, :], sb_state[:, :])

        if "khz" in dbg:
            t = mkdbg("khz", [128, 8192], BF16)
            with nc.Block() as b:
                @b.sync
                def _(e):
                    e.dma_start(out=t[:, :], in_=khz[:, :]).then_inc(ds.sem, 16)
                    e.wait_ge(ds.sem, ds.inc(16))
        if "Ssb" in dbg:
            t = mkdbg("Ssb", [128, 4096])
            with nc.Block() as b:
                @b.sync
                def _(e):
                    e.dma_start(out=t[:, :], in_=Ssb[:, :]).then_inc(ds.sem, 16)
                    e.wait_ge(ds.sem, ds.inc(16))
        with nc.Block() as b:
            @b.vector
            def _(e):
                for p in range(8 if p1 >= 3 else 0):
                    e.tensor_tensor_scan(scB[:, 0:32],
                                         fap(sb_small, C_ONES, [[128, 128], [0, 32]]),
                                         sb_csum[:, 32 * p:32 * (p + 1)], 0.0, ALU.mult, ALU.add).then_inc(sv.sem, 1)
                    e.wait_ge(sv.sem, sv.inc(1))
                    e.tensor_tensor(scA[:, 0:32], scB[:, 0:32],
                                    sb_csum[:, 32 * p:32 * (p + 1)], ALU.subtract)
                    e.drain()
                    e.tensor_copy(sb_dg[:, 32 * p:32 * (p + 1)], scA[:, 0:32])
                e.drain()
        if p1 >= 3:
            with nc.Block() as b:
                @b.scalar
                def _(e):
                    e.activation(sb_dg[:, 0:256], sb_dg[:, 0:256], AF.Exp)

        A_B1b.free()
        A_B1c = arena()
        sgath = A_B1c.sbuf_tensor("sgath", [128, 4 * 8 * 65], F32)
        A_B1c.allocate()

        with nc.Block() as b:
            @b.vector
            def _(e):
                for p in range(8 if p1 >= 3 else 0):
                    e.reduce_sum(slot("dcs"), sb_csum[:, 32 * p:32 * (p + 1)], AX)
                    e.drain()
                    e.tensor_copy(sb_ml[:, 32 + p:33 + p], slot("dcs"))
                e.drain()
        with nc.Block() as b:
            @b.scalar
            def _(e):
                if p1 >= 3:
                    e.activation(sb_ml[:, 40:48], sb_ml[:, 32:40], AF.Exp)
        with nc.Block() as b:
            @b.sync
            def _(e):
                for p in range(8 if p1 >= 3 else 0):
                    e.dma_start(out=st_in[128 * p:128 * (p + 1), 0:64],
                                in_=sb_state[:, 64 * p:64 * (p + 1)]).then_inc(ds.sem, 16)
                    e.dma_start(out=st_in[128 * p:128 * (p + 1), 64:65],
                                in_=sb_ml[:, 40 + p:41 + p],
                                allow_slow_non_contiguous=True).then_inc(ds.sem, 16)
                if p1 >= 3:
                    e.wait_ge(ds.sem, ds.inc(16 * 16))
        if p1 >= 3:
            with nc.Block() as b:
                @b.gpsimd
                def _(e):
                    e.collective_compute("AllGather", ALU.bypass,
                                         replica_groups=[[0, 1, 2, 3], [4, 5, 6, 7]],
                                         ins=[st_in[:, :]], outs=[st_g[:, :]]).then_inc(cc.sem, 1)
                    cc.inc(1)
                    e.wait_ge(cc.sem, cc.v)
        CC_ST = cc.v
        with nc.Block() as b:
            @b.sync
            def _(e):
                for i in range(4 if p1 >= 3 else 0):
                    for p in range(8):
                        e.dma_start(out=sgath[:, 65 * (8 * i + p):65 * (8 * i + p + 1)],
                                    in_=st_g[1024 * i + 128 * p:1024 * i + 128 * (p + 1), :]).then_inc(ds.sem, 16)
                if p1 >= 3:
                    e.wait_ge(ds.sem, ds.inc(32 * 16))
        with nc.Block() as b:
            @b.vector
            def _(e):
                dt_s = slot("st_dt")
                for i in range(4 if p1 >= 3 else 0):
                    fcol = sb_small[:, C_FLAGS + i:C_FLAGS + i + 1]
                    ocol = sb_small[:, C_OMF + i:C_OMF + i + 1]
                    for p in range(8):
                        u0 = 65 * (8 * i + p)
                        e.tensor_scalar(scA[:, 0:64], sgath[:, u0:u0 + 64], fcol, None, ALU.mult)
                        e.scalar_tensor_tensor(dt_s, sgath[:, u0 + 64:u0 + 65], fcol, ocol,
                                               ALU.mult, ALU.add)
                        e.drain()
                        e.scalar_tensor_tensor(sb_sin[:, 64 * p:64 * (p + 1)],
                                               sb_sin[:, 64 * p:64 * (p + 1)], dt_s,
                                               scA[:, 0:64], ALU.mult, ALU.add)
                        e.drain()

        for g in range(32 if p1 >= 3 else 0):
            s, c = g // 8, g % 8
            tt16 = (s * SBL + c * 64) // 128
            rbase = (64 * c) % 128
            with nc.Block() as b:
                @b.vector
                def _(e, s=s, c=c, g=g):
                    for p in range(8):
                        ec = 128 * s + 16 * p + 2 * c
                        e.tensor_scalar(sS2[:, 64 * p:64 * (p + 1)],
                                        sb_sin[:, 64 * p:64 * (p + 1)],
                                        sb_dg[:, 32 * p + g:32 * p + g + 1], None, ALU.mult)
                        qc0 = p * TC + s * SBL + c * 64
                        e.tensor_copy(sq1a[0:64, 64 * p:64 * p + 32], qh[0:64, qc0:qc0 + 32])
                        e.tensor_scalar(sq1a[0:64, 64 * p + 32:64 * p + 64],
                                        qh[0:64, qc0 + 32:qc0 + 64],
                                        sb_expSC[0:64, ec:ec + 1], None, ALU.mult)
                        e.tensor_copy(sq1b[64:128, 64 * p:64 * p + 32], qh[64:128, qc0:qc0 + 32])
                        e.tensor_scalar(sq1b[64:128, 64 * p + 32:64 * p + 64],
                                        qh[64:128, qc0 + 32:qc0 + 64],
                                        sb_expSC[64:128, ec:ec + 1], None, ALU.mult)
            with nc.Block() as b:
                @b.tensor
                def _(e, s=s, c=c, rbase=rbase):
                    for p in range(8):
                        pO = P[4 + p // 4]
                        fo = 128 * (p % 4)
                        for h2 in range(2):
                            sq = sq1a if h2 == 0 else sq1b
                            e.matmul(pO[rbase:rbase + 64, fo + 64 * h2:fo + 64 * h2 + 64],
                                     sq[:, 64 * p:64 * (p + 1)],
                                     sS2[:, 64 * p:64 * (p + 1)],
                                     start=True, stop=True)
            with nc.Block() as b:
                @b.vector
                def _(e, g=g, tt16=tt16, rbase=rbase):
                    for half in range(2):
                        e.tensor_copy(sPc[rbase:rbase + 64, half * 512:(half + 1) * 512],
                                      P[4 + half][rbase:rbase + 64, :])
                    e.drain()
                    for half in range(2):
                        dst = y0[rbase:rbase + 64, tt16 * 1024 + half * 512:tt16 * 1024 + (half + 1) * 512]
                        e.tensor_tensor(dst, dst, sPc[rbase:rbase + 64, half * 512:(half + 1) * 512], ALU.add)

        if "qh" in dbg:
            t = mkdbg("qh", [8 * 128, TC], BF16)
            with nc.Block() as b:
                @b.sync
                def _(e):
                    for k in range(8):
                        e.dma_start(out=t[k * 128:(k + 1) * 128, :],
                                    in_=qh[:, k * TC:(k + 1) * TC]).then_inc(ds.sem, 16)
                    e.wait_ge(ds.sem, ds.inc(8 * 16))
        if "expSC" in dbg:
            t = mkdbg("expSC", [128, 512])
            with nc.Block() as b:
                @b.sync
                def _(e):
                    e.dma_start(out=t[:, :], in_=sb_expSC[:, :]).then_inc(ds.sem, 16)
                    e.wait_ge(ds.sem, ds.inc(16))
        if "y0" in dbg:
            t = mkdbg("y0", [TC, D], BF16)
            with nc.Block() as b:
                @b.sync
                def _(e):
                    for k in range(16):
                        e.dma_start(out=t[k * 128:(k + 1) * 128, :],
                                    in_=y0[:, k * 1024:(k + 1) * 1024]).then_inc(ds.sem, 16)
                    e.wait_ge(ds.sem, ds.inc(16 * 16))
        if "state" in dbg:
            t = mkdbg("state", [128, 512])
            with nc.Block() as b:
                @b.sync
                def _(e):
                    e.dma_start(out=t[:, :], in_=sb_state[:, :]).then_inc(ds.sem, 16)
                    e.wait_ge(ds.sem, ds.inc(16))
        if "smallc" in dbg:
            t = mkdbg("smallc", [128, 128])
            with nc.Block() as b:
                @b.sync
                def _(e):
                    e.dma_start(out=t[:, :], in_=sb_small[:, :]).then_inc(ds.sem, 16)
                    e.wait_ge(ds.sem, ds.inc(16))
        if "sgath" in dbg:
            t = mkdbg("sgath", [128, 2080])
            with nc.Block() as b:
                @b.sync
                def _(e):
                    e.dma_start(out=t[:, :], in_=sgath[:, :]).then_inc(ds.sem, 16)
                    e.wait_ge(ds.sem, ds.inc(16))
        if "dg" in dbg:
            t = mkdbg("dg", [128, 256])
            with nc.Block() as b:
                @b.sync
                def _(e):
                    e.dma_start(out=t[:, :], in_=sb_dg[:, :]).then_inc(ds.sem, 16)
                    e.wait_ge(ds.sem, ds.inc(16))
        if "stg" in dbg:
            t = mkdbg("stg", [4 * D, E + 1])
            with nc.Block() as b:
                @b.sync
                def _(e):
                    e.dma_start(out=t[:, :], in_=st_g[:, :]).then_inc(ds.sem, 16)
                    e.wait_ge(ds.sem, ds.inc(16))
        if "sin" in dbg:
            t = mkdbg("sin", [128, 512])
            with nc.Block() as b:
                @b.sync
                def _(e):
                    e.dma_start(out=t[:, :], in_=sb_sin[:, :]).then_inc(ds.sem, 16)
                    e.wait_ge(ds.sem, ds.inc(16))

        A_B1c.free()
        A_B1a.free()

        OPROJ = p1 >= 4
        A_B2 = arena()
        scY = A_B2.sbuf_tensor("scY", [128, 1024], BF16)
        wsb2 = A_B2.sbuf_tensor("wsb2", [128, 8 * 1024], BF16)
        stg = A_B2.sbuf_tensor("stg", [128, 1024], F32)
        sidb = A_B2.sbuf_tensor("sidb", [128, 128], BF16)
        A_B2.allocate()
        with nc.Block() as b:
            @b.gpsimd
            def _(e):
                for kk in range(8 if OPROJ else 0):
                    e.dma_start(out=wsb2[:, kk * 1024:(kk + 1) * 1024],
                                in_=ow_t[kk * 128:(kk + 1) * 128, :]).then_inc(dg.sem, 16)
                    dg.inc(16)
                e.wait_ge(dg.sem, dg.v)
            @b.vector
            def _(e):
                if OPROJ:
                    e.tensor_copy(sidb[:, :], sb_idf[:, :])
        for t16 in range(16 if OPROJ else 0):
            with nc.Block() as b:
                @b.tensor
                def _(e, t16=t16):
                    for j in range(8):
                        e.matmul(P[j // 4][0:128, 128 * (j % 4):128 * (j % 4) + 128],
                                 y0[:, t16 * 1024 + j * 128:t16 * 1024 + (j + 1) * 128],
                                 sidb[:, :], start=True, stop=True)
            with nc.Block() as b:
                @b.vector
                def _(e, t16=t16):
                    e.tensor_copy(scY[:, 0:512], P[0][:, :])
                    e.tensor_copy(scY[:, 512:1024], P[1][:, :])
            for og in range(2):
                with nc.Block() as b:
                    @b.tensor
                    def _(e, t16=t16, og=og):
                        for o4 in range(4):
                            o = og * 4 + o4
                            for hf in range(8):
                                e.matmul(P[4 + o4][:, 0:128],
                                         wsb2[:, hf * 1024 + o * 128:hf * 1024 + (o + 1) * 128],
                                         scY[:, hf * 128:(hf + 1) * 128],
                                         start=(hf == 0), stop=(hf == 7))
                with nc.Block() as b:
                    @b.vector
                    def _(e, t16=t16, og=og):
                        for o4 in range(4):
                            o = og * 4 + o4
                            e.tensor_copy(stg[:, o * 128:(o + 1) * 128], P[4 + o4][:, 0:128])
            with nc.Block() as b:
                @b.sync
                def _(e, t16=t16):
                    e.dma_start(out=fap(attn0_st.tensor, 128 * t16,
                                        [[TC, 128], [128 * TC, 8], [1, 128]]),
                                in_=fap(stg, 0, [[1024, 128], [128, 8], [1, 128]])).then_inc(ds.sem, 16)
                    e.wait_ge(ds.sem, ds.inc(16))
        A_B2.free()
        A_y0.free()
        if "attn0" in dbg:
            t = mkdbg("attn0", [D, TC])
            with nc.Block() as b:
                @b.sync
                def _(e):
                    e.dma_start(out=t[:, :], in_=attn0_st[:, :]).then_inc(ds.sem, 16)
                    e.wait_ge(ds.sem, ds.inc(16))

    if stage >= 2:
        A_C = arena()
        sb_hh = A_C.sbuf_tensor("sb_hh", [128, NT0 * TC], BF16)
        sb_xq = A_C.sbuf_tensor("sb_xq", [128, 8 * TC], BF16)
        scE = A_C.sbuf_tensor("scE", [128, 2 * 512], F32)
        scF = A_C.sbuf_tensor("scF", [128, 1024], F32)
        wti = A_C.sbuf_tensor("wti", [128, 4 * 1024], BF16)
        scQ = A_C.sbuf_tensor("scQ", [128, 2 * 512], BF16)
        scX = A_C.sbuf_tensor("scX", [128, 8 * 512], F32)
        wtiR = A_C.sbuf_tensor("wtiR", [128, 2 * 1024], BF16)
        A_C.allocate()

        with nc.Block() as b:
            @b.sync
            def _(e):
                e.wait_ge(cc.sem, CC_AR0)
                e.dma_start(out=fap(sb_srow, 132, [[192, 1], [1, 2]]), in_=AR_OUT[0][:, :]).then_inc(ds.sem, 16)
                e.wait_ge(ds.sem, ds.inc(16))
        with nc.Block() as b:
            @b.scalar
            def _(e):
                e.activation(fap(sb_srow, 134, [[192, 1], [1, 1]]), sb_small[0:1, C_CLIPX:C_CLIPX + 1], AF.Sigmoid)
        with nc.Block() as b:
            @b.vector
            def _(e):
                e.tensor_tensor(fap(sb_srow, 135, [[192, 1], [1, 1]]), fap(sb_srow, 134, [[192, 1], [1, 1]]),
                                fap(sb_srow, 132, [[192, 1], [1, 1]]), ALU.mult)
                e.drain()
                e.tensor_scalar(fap(sb_srow, 135, [[192, 1], [1, 1]]), fap(sb_srow, 135, [[192, 1], [1, 1]]),
                                1.0 / 127.0, None, ALU.mult)
                e.drain()
                e.reciprocal(fap(sb_srow, 136, [[192, 1], [1, 1]]), fap(sb_srow, 135, [[192, 1], [1, 1]]))
                e.drain()
        with nc.Block() as b:
            @b.sync
            def _(e):
                e.dma_start(out=stats_dram[0:1, 0:3],
                            in_=fap(sb_srow, 134, [[192, 1], [1, 3]])).then_inc(ds.sem, 16)
                e.wait_ge(ds.sem, ds.inc(16))
                e.dma_start(out=slot("rsx0"),
                            in_=fap(stats_dram.tensor, 2, [[0, 128], [1, 1]])).then_inc(ds.sem, 16)
                e.dma_start(out=slot("sx0"),
                            in_=fap(stats_dram.tensor, 1, [[0, 128], [1, 1]])).then_inc(ds.sem, 16)
                e.wait_ge(ds.sem, ds.inc(32))
        with nc.Block() as b:
            @b.vector
            def _(e):
                for k in range(8):
                    for hh in range(2):
                        c0 = k * TC + hh * 1024
                        e.tensor_scalar(scF[:, :], sb_xn[:, c0:c0 + 1024],
                                        slot("rsx0"), MAGIC, ALU.mult, ALU.add)
                        e.drain()
                        e.tensor_scalar(scF[:, :], scF[:, :], -MAGIC, None, ALU.add)
                        e.drain()
                        e.tensor_scalar(sb_xq[:, c0:c0 + 1024], scF[:, :],
                                        127.0, -128.0, ALU.min, ALU.max)
                        e.drain()
        if "xq0" in dbg:
            dump_fm("xq0", sb_xq)

        with nc.Block() as b:
            @b.sync
            def _(e):
                e.wait_ge(cc.sem, CC_W)
        amax_h = slot("h_amax")
        tmp_h = slot("h_tmpm")
        with nc.Block() as b:
            dsb0, dwb0 = ds.v, dw.v
            @b.sync
            def _(e):
                tb = st.v
                for it in range(NT0):
                    if it >= 2:
                        e.wait_ge(st.sem, tb + it - 1)
                    r, sl_ = it % 8, it // 8
                    pc = ds if it % 2 == 0 else dw
                    for half, base in ((0, WB_L0G), (1, WB_L0U)):
                        e.dma_start(
                            out=wti[:, (it % 2) * 2048 + half * 1024:(it % 2) * 2048 + (half + 1) * 1024],
                            in_=fap(wb_g.tensor, (r * WB_ROWS + base + sl_ * 1024) * 128,
                                    [[1024, 128], [1, 1024]])).then_inc(pc.sem, 16)
                        pc.inc(16)
            @b.tensor
            def _(e):
                svb = sv.v
                for it in range(NT0):
                    pb, pv = (dsb0, ds) if it % 2 == 0 else (dwb0, dw)
                    e.wait_ge(pv.sem, pb + (it // 2 + 1) * 2 * 16)
                    if it >= 1:
                        e.wait_ge(sv.sem, svb + it * 4)
                    wof = (it % 2) * 2048
                    for tg in range(4):
                        for dk in range(8):
                            e.matmul(P[tg][:, :],
                                     wti[:, wof + dk * 128:wof + (dk + 1) * 128],
                                     sb_xq[:, dk * TC + tg * 512:dk * TC + (tg + 1) * 512],
                                     start=(dk == 0), stop=(dk == 7))
                            mm = e.matmul(P[4 + tg][:, :],
                                     wti[:, wof + 1024 + dk * 128:wof + 1024 + (dk + 1) * 128],
                                     sb_xq[:, dk * TC + tg * 512:dk * TC + (tg + 1) * 512],
                                     start=(dk == 0), stop=(dk == 7))
                    mm.then_inc(st.sem, 1)
                    st.inc(1)
            @b.scalar
            def _(e):
                stb = st.v - NT0
                svb = sv.v
                for it in range(NT0):
                    e.wait_ge(st.sem, stb + it + 1)
                    for tg in range(4):
                        if it * 4 + tg >= 2:
                            e.wait_ge(sv.sem, svb + it * 4 + tg - 1)
                        e.activation(scE[:, (tg % 2) * 512:(tg % 2 + 1) * 512], P[tg][:, :],
                                     AF.Silu, scale=slot("sx0")).then_inc(ss.sem, 1)
                        ss.inc(1)
            dyb2, dxb2 = dy.v, dx.v
            @b.gpsimd
            def _(e, dyb2=dyb2, dxb2=dxb2):
                for it in range(NT0):
                    if it >= 2:
                        e.wait_ge(dx.sem, dxb2 + (it - 1) * 16)
                    e.dma_start(out=wtiR[:, (it % 2) * 1024:(it % 2 + 1) * 1024],
                                in_=fap(wb_g.tensor, WB_L0D * 128 + it * 128,
                                        [[I0, 128], [WB_ROWS * 128, 8], [1, 128]])).then_inc(dy.sem, 16)
                    dy.inc(16)
                    e.wait_ge(dy.sem, dyb2 + (it + 1) * 16)
                    e.dma_start(out=fap(dwn0.tensor, it * 128 * 1024, [[1024, 128], [1, 1024]]),
                                in_=wtiR[:, (it % 2) * 1024:(it % 2 + 1) * 1024]).then_inc(dx.sem, 16)
                    dx.inc(16)
                e.wait_ge(dx.sem, dx.v)
            @b.vector
            def _(e):
                ssb2 = ss.v - NT0 * 4
                for it in range(NT0):
                    for tg in range(4):
                        e.wait_ge(ss.sem, ssb2 + it * 4 + tg + 1)
                        e.scalar_tensor_tensor(sb_hh[:, it * TC + tg * 512:it * TC + (tg + 1) * 512],
                                                P[4 + tg][:, :], slot("sx0"),
                                                scE[:, (tg % 2) * 512:(tg % 2 + 1) * 512],
                                                ALU.mult, ALU.mult)
                        e.drain()
                        e.reduce_max(tmp_h, sb_hh[:, it * TC + tg * 512:it * TC + (tg + 1) * 512],
                                     AX, apply_absolute_value=True)
                        e.drain()
                        if it == 0 and tg == 0:
                            e.tensor_copy(amax_h, tmp_h).then_inc(sv.sem, 1)
                        else:
                            e.tensor_tensor(amax_h, amax_h, tmp_h, ALU.max).then_inc(sv.sem, 1)
                        sv.inc(1)
        with nc.Block() as b:
            @b.sync
            def _(e):
                e.dma_start(out=fap(sb_srow, 0, [[192, 1], [1, 128]]),
                            in_=fap(sb_small, SLOT["h_amax"], [[128, 128], [1, 1]])).then_inc(ds.sem, 16)
                e.wait_ge(ds.sem, ds.inc(16))
        with nc.Block() as b:
            @b.vector
            def _(e):
                e.reduce_max(fap(sb_srow, 128, [[192, 1], [1, 1]]), fap(sb_srow, 0, [[192, 1], [1, 128]]), AX)
                e.drain()
                e.tensor_copy(fap(sb_srow, 129, [[192, 1], [1, 1]]), fap(sb_srow, 128, [[192, 1], [1, 1]]))
        with nc.Block() as b:
            @b.sync
            def _(e):
                e.dma_start(out=AR_IN[1][:, :], in_=fap(sb_srow, 128, [[192, 1], [1, 2]])).then_inc(ds.sem, 16)
                e.wait_ge(ds.sem, ds.inc(16))
        with nc.Block() as b:
            @b.gpsimd
            def _(e):
                e.collective_compute("AllReduce", ALU.max,
                                     replica_groups=[list(range(8))],
                                     ins=[AR_IN[1][:, :]], outs=[AR_OUT[1][:, :]]).then_inc(cc.sem, 1)
                cc.inc(1)
        CC_AR1 = cc.v
        with nc.Block() as b:
            @b.sync
            def _(e):
                e.wait_ge(cc.sem, CC_AR1)
                e.dma_start(out=fap(sb_srow, 132, [[192, 1], [1, 2]]), in_=AR_OUT[1][:, :]).then_inc(ds.sem, 16)
                e.wait_ge(ds.sem, ds.inc(16))
        with nc.Block() as b:
            @b.scalar
            def _(e):
                e.activation(fap(sb_srow, 134, [[192, 1], [1, 1]]), sb_small[0:1, C_CLIPD:C_CLIPD + 1], AF.Sigmoid)
        with nc.Block() as b:
            @b.vector
            def _(e):
                e.tensor_tensor(fap(sb_srow, 135, [[192, 1], [1, 1]]), fap(sb_srow, 134, [[192, 1], [1, 1]]),
                                fap(sb_srow, 132, [[192, 1], [1, 1]]), ALU.mult)
                e.drain()
                e.tensor_scalar(fap(sb_srow, 135, [[192, 1], [1, 1]]), fap(sb_srow, 135, [[192, 1], [1, 1]]),
                                1.0 / 127.0, None, ALU.mult)
                e.drain()
                e.reciprocal(fap(sb_srow, 136, [[192, 1], [1, 1]]), fap(sb_srow, 135, [[192, 1], [1, 1]]))
                e.drain()
        with nc.Block() as b:
            @b.sync
            def _(e):
                e.dma_start(out=stats_dram[0:1, 0:3],
                            in_=fap(sb_srow, 134, [[192, 1], [1, 3]])).then_inc(ds.sem, 16)
                e.wait_ge(ds.sem, ds.inc(16))
                e.dma_start(out=slot("rsh0"),
                            in_=fap(stats_dram.tensor, 2, [[0, 128], [1, 1]])).then_inc(ds.sem, 16)
                e.dma_start(out=slot("sh0"),
                            in_=fap(stats_dram.tensor, 1, [[0, 128], [1, 1]])).then_inc(ds.sem, 16)
                e.wait_ge(ds.sem, ds.inc(32))

        t_xa = mkdbg("xa0", [128, 8 * TC]) if "xa0" in dbg else None
        t_hq = mkdbg("hq0", [128, 4 * NT0 * 512], BF16) if "hq0" in dbg else None
        for tg in range(4):
            if serial_down:
                for it in range(NT0):
                    with nc.Block() as b:
                        @b.sync
                        def _(e, it=it):
                            e.dma_start(
                                out=wti[:, (it % 2) * 1024:(it % 2 + 1) * 1024],
                                in_=fap(dwn0.tensor, it * 128 * 1024,
                                        [[1024, 128], [1, 1024]])).then_inc(ds.sem, 16)
                            e.wait_ge(ds.sem, ds.inc(16))
                        @b.vector
                        def _(e, tg=tg, it=it):
                            e.tensor_scalar(scE[:, 0:512], sb_hh[:, it * TC + tg * 512:it * TC + (tg + 1) * 512],
                                            slot("rsh0"), MAGIC, ALU.mult, ALU.add)
                            e.drain()
                            e.tensor_scalar(scE[:, 0:512], scE[:, 0:512], -MAGIC, None, ALU.add)
                            e.drain()
                            e.tensor_scalar(scQ[:, (it % 2) * 512:(it % 2 + 1) * 512], scE[:, 0:512],
                                            127.0, -128.0, ALU.min, ALU.max)
                            e.drain()
                    with nc.Block() as b:
                        @b.tensor
                        def _(e, it=it):
                            for o in range(8):
                                e.matmul(P[o][:, :],
                                         wti[:, (it % 2) * 1024 + o * 128:(it % 2) * 1024 + (o + 1) * 128],
                                         scQ[:, (it % 2) * 512:(it % 2 + 1) * 512],
                                         start=(it == 0), stop=(it == NT0 - 1))
            if serial_down:
                pass
            else:
              with nc.Block() as b:
                st_b, ds_b, sv_b, dw_b = st.v, ds.v, sv.v, dw.v
                @b.sync
                def _(e, tg=tg, st_b=st_b):
                    for it in range(NT0):
                        if it >= 2:
                            e.wait_ge(st.sem, st_b + it - 1)
                        pc = ds if it % 2 == 0 else dw
                        e.dma_start(
                            out=wti[:, (it % 2) * 1024:(it % 2 + 1) * 1024],
                            in_=fap(dwn0.tensor, it * 128 * 1024,
                                    [[1024, 128], [1, 1024]])).then_inc(pc.sem, 16)
                        pc.inc(16)
                @b.vector
                def _(e, tg=tg, st_b=st_b):
                    for it in range(NT0):
                        if it >= 2:
                            e.wait_ge(st.sem, st_b + it - 1)
                        e.tensor_scalar(scE[:, 0:512], sb_hh[:, it * TC + tg * 512:it * TC + (tg + 1) * 512],
                                        slot("rsh0"), MAGIC, ALU.mult, ALU.add)
                        e.drain()
                        e.tensor_scalar(scE[:, 0:512], scE[:, 0:512], -MAGIC, None, ALU.add)
                        e.drain()
                        e.tensor_scalar(scQ[:, (it % 2) * 512:(it % 2 + 1) * 512], scE[:, 0:512],
                                        127.0, -128.0, ALU.min, ALU.max).then_inc(sv.sem, 1)
                        sv.inc(1)
                if t_hq is not None:
                    @b.scalar
                    def _(e, tg=tg, sv_b=sv_b):
                        for it in range(NT0):
                            e.wait_ge(sv.sem, sv_b + it + 1)
                            e.dma_start(out=t_hq[:, (tg * NT0 + it) * 512:(tg * NT0 + it + 1) * 512],
                                        in_=scQ[:, (it % 2) * 512:(it % 2 + 1) * 512]).then_inc(da.sem, 16)
                            da.inc(16)
                        e.wait_ge(da.sem, da.v)
                @b.tensor
                def _(e, tg=tg, ds_b=ds_b, sv_b=sv_b, dw_b=dw_b):
                    for it in range(NT0):
                        pb, pv = (ds_b, ds) if it % 2 == 0 else (dw_b, dw)
                        e.wait_ge(pv.sem, pb + (it // 2 + 1) * 16)
                        e.wait_ge(sv.sem, sv_b + it + 1)
                        for o in range(8):
                            mm = e.matmul(P[o][:, :],
                                     wti[:, (it % 2) * 1024 + o * 128:(it % 2) * 1024 + (o + 1) * 128],
                                     scQ[:, (it % 2) * 512:(it % 2 + 1) * 512],
                                     start=(it == 0), stop=(it == NT0 - 1))
                        mm.then_inc(st.sem, 1)
                        st.inc(1)
            if serial_asm:
                with nc.Block() as b:
                    @b.sync
                    def _(e, tg=tg):
                        for o in range(8):
                            e.dma_start(out=scX[:, o * 512:(o + 1) * 512],
                                        in_=x_in[o * 128:(o + 1) * 128, tg * 512:(tg + 1) * 512]).then_inc(ds.sem, 16)
                            ds.inc(16)
                        e.wait_ge(ds.sem, ds.v)
                for half in range(2):
                    with nc.Block() as b:
                        @b.sync
                        def _(e, tg=tg, half=half):
                            for j in range(4):
                                o = half * 4 + j
                                e.dma_start(out=scF[:, j * 512:(j + 1) * 512],
                                            in_=attn0_st[o * 128:(o + 1) * 128, tg * 512:(tg + 1) * 512]).then_inc(ds.sem, 16)
                                ds.inc(16)
                            e.wait_ge(ds.sem, ds.v)
                    with nc.Block() as b:
                        @b.vector
                        def _(e, tg=tg, half=half):
                            for j in range(4):
                                o = half * 4 + j
                                e.tensor_tensor(scX[:, o * 512:(o + 1) * 512],
                                                scX[:, o * 512:(o + 1) * 512],
                                                scF[:, j * 512:(j + 1) * 512], ALU.add)
                                e.drain()
                                e.scalar_tensor_tensor(scX[:, o * 512:(o + 1) * 512],
                                                       P[o][:, :], slot("sh0"),
                                                       scX[:, o * 512:(o + 1) * 512],
                                                       ALU.mult, ALU.add)
                                e.drain()
                with nc.Block() as b:
                    @b.scalar
                    def _(e, tg=tg):
                        for o in range(8):
                            e.dma_start(out=x1_st[o * 128:(o + 1) * 128, tg * 512:(tg + 1) * 512],
                                        in_=scX[:, o * 512:(o + 1) * 512]).then_inc(da.sem, 16)
                            da.inc(16)
                        e.wait_ge(da.sem, da.v)
            else:
              with nc.Block() as b:
                ds_b, sv_b, dg_b, dx_b = ds.v, sv.v, dg.v, dx.v
                @b.sync
                def _(e, tg=tg, sv_b=sv_b):
                    for o in range(8):
                        e.dma_start(out=scX[:, o * 512:(o + 1) * 512],
                                    in_=x_in[o * 128:(o + 1) * 128, tg * 512:(tg + 1) * 512]).then_inc(ds.sem, 16)
                        ds.inc(16)
                    for o in range(8):
                        pc = dg if o % 2 == 0 else dx
                        if o >= 2:
                            e.wait_ge(sv.sem, sv_b + o - 1)
                        e.dma_start(out=scE[:, (o % 2) * 512:(o % 2 + 1) * 512],
                                    in_=attn0_st[o * 128:(o + 1) * 128, tg * 512:(tg + 1) * 512]).then_inc(pc.sem, 16)
                        pc.inc(16)
                @b.vector
                def _(e, tg=tg, ds_b=ds_b, dg_b=dg_b, dx_b=dx_b):
                    for o in range(8):
                        if o == 0:
                            e.wait_ge(ds.sem, ds_b + 8 * 16)
                        pb, pv = (dg_b, dg) if o % 2 == 0 else (dx_b, dx)
                        e.wait_ge(pv.sem, pb + (o // 2 + 1) * 16)
                        e.tensor_tensor(scX[:, o * 512:(o + 1) * 512],
                                        scX[:, o * 512:(o + 1) * 512],
                                        scE[:, (o % 2) * 512:(o % 2 + 1) * 512], ALU.add)
                        e.drain()
                        e.scalar_tensor_tensor(scX[:, o * 512:(o + 1) * 512],
                                               P[o][:, :], slot("sh0"),
                                               scX[:, o * 512:(o + 1) * 512],
                                               ALU.mult, ALU.add)
                        e.drain()
                        e.tensor_copy(slot("asmtk"), ones_col).then_inc(sv.sem, 1)
                        sv.inc(1)
                @b.scalar
                def _(e, tg=tg, sv_b=sv_b):
                    for o in range(8):
                        e.wait_ge(sv.sem, sv_b + o + 1)
                        e.dma_start(out=x1_st[o * 128:(o + 1) * 128, tg * 512:(tg + 1) * 512],
                                    in_=scX[:, o * 512:(o + 1) * 512]).then_inc(da.sem, 16)
                        da.inc(16)
                    e.wait_ge(da.sem, da.v)
        if "x1" in dbg:
            t = mkdbg("x1", [D, TC])
            with nc.Block() as b:
                @b.sync
                def _(e):
                    e.dma_start(out=t[:, :], in_=x1_st[:, :]).then_inc(ds.sem, 16)
                    e.wait_ge(ds.sem, ds.inc(16))
        if "h0t" in dbg:
            t = mkdbg("h0t", [128, TC], BF16)
            with nc.Block() as b:
                @b.sync
                def _(e):
                    e.dma_start(out=t[:, :], in_=sb_hh[:, 0:TC]).then_inc(ds.sem, 16)
                    e.wait_ge(ds.sem, ds.inc(16))
        A_C.free()

    if stage >= 3:
        CC_AR2 = layer_norm(x1_st, sb_xn, 2, "ln1")
        dump_fm("xn1", sb_xn)

        A_D1 = arena()
        q1 = A_D1.sbuf_tensor("q1", [128, 8 * TC], BF16)
        A_D1.allocate()
        A_D1b = arena()
        xq1 = A_D1b.sbuf_tensor("xq1", [128, 8 * TC], BF16)
        A_D1b.allocate()
        A_D2 = arena()
        wD = A_D2.sbuf_tensor("wD", [128, 8 * 1024], BF16)
        bvB = A_D2.sbuf_tensor("bvB", [128, 1024], F32)
        scD = A_D2.sbuf_tensor("scD", [128, 4096], BF16)
        scF1 = A_D2.sbuf_tensor("scF1", [128, 2048], F32)
        wti1 = A_D2.sbuf_tensor("wti1", [128, 2 * 2048], BF16)
        wtiR1 = A_D2.sbuf_tensor("wtiR1", [128, 2 * 1024], BF16)
        scE1 = A_D2.sbuf_tensor("scE1", [128, 2 * 512], F32)
        A_D2.allocate()

        with nc.Block() as b:
            @b.sync
            def _(e):
                e.dma_start(out=sb_small[:, 56:64],
                            in_=fap(bq_c.tensor, 0, [[1, 128], [128, 8]]),
                            allow_slow_non_contiguous=True).then_inc(ds.sem, 16)
                e.dma_start(out=sb_small[:, 64:72],
                            in_=fap(bk_c.tensor, 0, [[1, 128], [128, 8]]),
                            allow_slow_non_contiguous=True).then_inc(ds.sem, 16)
                e.dma_start(out=sb_small[:, 96:104],
                            in_=fap(bo_c.tensor, 0, [[1, 128], [128, 8]]),
                            allow_slow_non_contiguous=True).then_inc(ds.sem, 16)
                e.dma_start(out=bvB[:, :],
                            in_=fap(bv_row.tensor, 0, [[0, 128], [1, 1024]])).then_inc(ds.sem, 16)
                e.wait_ge(ds.sem, ds.inc(4 * 16))

        for wi, w_ap, bcol in ((0, Wq_t, 56), (1, Wk_t, 64)):
            with nc.Block() as b:
                @b.gpsimd
                def _(e, w_ap=w_ap):
                    for kk in range(8):
                        e.dma_start(out=wD[:, kk * 1024:(kk + 1) * 1024],
                                    in_=w_ap[kk * 128:(kk + 1) * 128, :]).then_inc(dg.sem, 16)
                        dg.inc(16)
                    e.wait_ge(dg.sem, dg.v)
            for tg in range(4):
                with nc.Block() as b:
                    @b.tensor
                    def _(e, tg=tg):
                        for o in range(8):
                            for dk in range(8):
                                e.matmul(P[o][:, :],
                                         wD[:, dk * 1024 + o * 128:dk * 1024 + (o + 1) * 128],
                                         sb_xn[:, dk * TC + tg * 512:dk * TC + (tg + 1) * 512],
                                         start=(dk == 0), stop=(dk == 7))
                with nc.Block() as b:
                    @b.vector
                    def _(e, tg=tg, wi=wi, bcol=bcol):
                        for o in range(8):
                            if wi == 0:
                                e.tensor_scalar(q1[:, o * TC + tg * 512:o * TC + (tg + 1) * 512],
                                                P[o][:, :], sb_small[:, bcol + o:bcol + o + 1],
                                                None, ALU.add)
                            else:
                                e.tensor_scalar(scD[:, o * 512:(o + 1) * 512],
                                                P[o][:, :], sb_small[:, bcol + o:bcol + o + 1],
                                                None, ALU.add)
                if wi == 1:
                    with nc.Block() as b:
                        @b.sync
                        def _(e, tg=tg):
                            for o in range(8):
                                e.dma_start(out=kvk_in[o * 128:(o + 1) * 128, tg * 512:(tg + 1) * 512],
                                            in_=scD[:, o * 512:(o + 1) * 512]).then_inc(ds.sem, 16)
                            e.wait_ge(ds.sem, ds.inc(8 * 16))

        with nc.Block() as b:
            @b.gpsimd
            def _(e):
                for kk in range(8):
                    e.dma_start(out=wD[:, kk * 1024:(kk + 1) * 1024],
                                in_=Wv_t[kk * 128:(kk + 1) * 128, :]).then_inc(dg.sem, 16)
                    dg.inc(16)
                e.wait_ge(dg.sem, dg.v)
        for tg in range(4):
            with nc.Block() as b:
                @b.tensor
                def _(e, tg=tg):
                    for q4 in range(4):
                        for hf in range(2):
                            for dk in range(8):
                                e.matmul(P[2 * q4 + hf][:, :],
                                         sb_xn[:, dk * TC + tg * 512 + q4 * 128:dk * TC + tg * 512 + (q4 + 1) * 128],
                                         wD[:, dk * 1024 + hf * 512:dk * 1024 + (hf + 1) * 512],
                                         start=(dk == 0), stop=(dk == 7))
            with nc.Block() as b:
                @b.vector
                def _(e, tg=tg):
                    for q4 in range(4):
                        for hf in range(2):
                            e.tensor_tensor(scD[:, q4 * 1024 + hf * 512:q4 * 1024 + (hf + 1) * 512],
                                            P[2 * q4 + hf][:, :], bvB[:, hf * 512:(hf + 1) * 512],
                                            ALU.add)
            with nc.Block() as b:
                @b.sync
                def _(e, tg=tg):
                    for q4 in range(4):
                        e.dma_start(out=kvv_in[tg * 512 + q4 * 128:tg * 512 + (q4 + 1) * 128, :],
                                    in_=scD[:, q4 * 1024:(q4 + 1) * 1024]).then_inc(ds.sem, 16)
                    e.wait_ge(ds.sem, ds.inc(4 * 16))
        with nc.Block() as b:
            @b.gpsimd
            def _(e):
                e.collective_compute("AllGather", ALU.bypass,
                                     replica_groups=[[0, 1, 2, 3], [4, 5, 6, 7]],
                                     ins=[kvk_in[:, :]], outs=[kvk_g[:, :]]).then_inc(cc.sem, 1)
                cc.inc(1)
                e.collective_compute("AllGather", ALU.bypass,
                                     replica_groups=[[0, 1, 2, 3], [4, 5, 6, 7]],
                                     ins=[kvv_in[:, :]], outs=[kvv_g[:, :]]).then_inc(cc.sem, 1)
                cc.inc(1)
        CC_KV = cc.v

        with nc.Block() as b:
            @b.sync
            def _(e):
                e.wait_ge(cc.sem, CC_AR2)
                e.dma_start(out=fap(sb_srow, 132, [[192, 1], [1, 2]]),
                            in_=AR_OUT[2][:, :]).then_inc(ds.sem, 16)
                e.wait_ge(ds.sem, ds.inc(16))
        with nc.Block() as b:
            @b.vector
            def _(e):
                r132 = fap(sb_srow, 132, [[192, 1], [1, 1]])
                r135 = fap(sb_srow, 135, [[192, 1], [1, 1]])
                r136 = fap(sb_srow, 136, [[192, 1], [1, 1]])
                e.tensor_scalar(r135, r132, 1.0 / 32767.0, 1e-12, ALU.mult, ALU.max)
                e.drain()
                e.reciprocal(r136, r135)
                e.drain()
        with nc.Block() as b:
            @b.sync
            def _(e):
                e.dma_start(out=stats_dram[0:1, 0:3],
                            in_=fap(sb_srow, 134, [[192, 1], [1, 3]])).then_inc(ds.sem, 16)
                e.wait_ge(ds.sem, ds.inc(16))
                e.dma_start(out=slot("rsx1"),
                            in_=fap(stats_dram.tensor, 2, [[0, 128], [1, 1]])).then_inc(ds.sem, 16)
                e.dma_start(out=slot("sx1"),
                            in_=fap(stats_dram.tensor, 1, [[0, 128], [1, 1]])).then_inc(ds.sem, 16)
                e.wait_ge(ds.sem, ds.inc(32))
        with nc.Block() as b:
            @b.vector
            def _(e):
                for k in range(8):
                    e.tensor_scalar(scF1[:, 0:TC], sb_xn[:, k * TC:(k + 1) * TC],
                                    slot("rsx1"), MAGIC, ALU.mult, ALU.add)
                    e.drain()
                    e.tensor_scalar(xq1[:, k * TC:(k + 1) * TC], scF1[:, 0:TC],
                                    -MAGIC, None, ALU.add)
                    e.drain()

        with nc.Block() as b:
            @b.sync
            def _(e):
                e.wait_ge(cc.sem, CC_W)
        amax_g = slot("g_amax"); amax_u = slot("u_amax")
        with nc.Block() as b:
            st_b, ds_b, sv_b, ss_b, dg_b, dw_b = st.v, ds.v, sv.v, ss.v, dg.v, dw.v
            @b.sync
            def _(e, st_b=st_b):
                for it in range(NT1):
                    if it >= 2:
                        e.wait_ge(st.sem, st_b + it - 1)
                    r, sl_ = it % 8, it // 8
                    pc = ds if it % 2 == 0 else dw
                    for half, base in ((0, WB_L1G), (1, WB_L1U)):
                        e.dma_start(
                            out=wti1[:, (it % 2) * 2048 + half * 1024:(it % 2) * 2048 + (half + 1) * 1024],
                            in_=fap(wb_g.tensor, (r * WB_ROWS + base + sl_ * 1024) * 128,
                                    [[1024, 128], [1, 1024]])).then_inc(pc.sem, 16)
                        pc.inc(16)
            @b.tensor
            def _(e, ds_b=ds_b, sv_b=sv_b, dw_b=dw_b):
                for it in range(NT1):
                    pb, pv = (ds_b, ds) if it % 2 == 0 else (dw_b, dw)
                    e.wait_ge(pv.sem, pb + (it // 2 + 1) * 2 * 16)
                    if it >= 1:
                        e.wait_ge(sv.sem, sv_b + it * 8)
                    wof = (it % 2) * 2048
                    for tg in range(4):
                        for dk in range(8):
                            e.matmul(P[tg][:, :],
                                     wti1[:, wof + dk * 128:wof + (dk + 1) * 128],
                                     xq1[:, dk * TC + tg * 512:dk * TC + (tg + 1) * 512],
                                     start=(dk == 0), stop=(dk == 7))
                            mm = e.matmul(P[4 + tg][:, :],
                                     wti1[:, wof + 1024 + dk * 128:wof + 1024 + (dk + 1) * 128],
                                     xq1[:, dk * TC + tg * 512:dk * TC + (tg + 1) * 512],
                                     start=(dk == 0), stop=(dk == 7))
                    mm.then_inc(st.sem, 1)
                    st.inc(1)
            dab2, dyb2 = da.v, dy.v
            @b.scalar
            def _(e, dab2=dab2, dyb2=dyb2):
                for it in range(NT1):
                    if it >= 2:
                        e.wait_ge(dy.sem, dyb2 + (it - 1) * 16)
                    e.dma_start(out=wtiR1[:, (it % 2) * 1024:(it % 2 + 1) * 1024],
                                in_=fap(wb_g.tensor, WB_L1D * 128 + it * 128,
                                        [[I1, 128], [WB_ROWS * 128, 8], [1, 128]])).then_inc(da.sem, 16)
                    da.inc(16)
                    e.wait_ge(da.sem, dab2 + (it + 1) * 16)
                    e.dma_start(out=fap(dwn1.tensor, it * 128 * 1024, [[1024, 128], [1, 1024]]),
                                in_=wtiR1[:, (it % 2) * 1024:(it % 2 + 1) * 1024]).then_inc(dy.sem, 16)
                    dy.inc(16)
                e.wait_ge(dy.sem, dy.v)
            dg_b2, dx_b2 = dg.v, dx.v
            @b.vector
            def _(e, st_b=st_b, dg_b2=dg_b2, dx_b2=dx_b2):
                for it in range(NT1):
                    e.wait_ge(st.sem, st_b + it + 1)
                    for tg in range(4):
                        for half in range(2):
                            n = it * 8 + tg * 2 + half
                            if n >= 2:
                                pb, pv = (dg_b2, dg) if n % 2 == 0 else (dx_b2, dx)
                                e.wait_ge(pv.sem, pb + (n // 2) * 16)
                            pp = P[tg] if half == 0 else P[4 + tg]
                            am = amax_g if half == 0 else amax_u
                            e.tensor_scalar(scE1[:, (n % 2) * 512:(n % 2 + 1) * 512],
                                            pp[:, :], slot("sx1"), None, ALU.mult)
                            e.drain()
                            e.reduce_max(slot("rawm"), scE1[:, (n % 2) * 512:(n % 2 + 1) * 512],
                                         AX, apply_absolute_value=True)
                            e.drain()
                            if it == 0 and tg == 0:
                                e.tensor_copy(am, slot("rawm"))
                            else:
                                e.tensor_tensor(am, am, slot("rawm"), ALU.max)
                            e.drain()
                            e.nop().then_inc(sv.sem, 1)
                            sv.inc(1)
            @b.gpsimd
            def _(e, sv_b=sv_b):
                for it in range(NT1):
                    for tg in range(4):
                        for half in range(2):
                            n = it * 8 + tg * 2 + half
                            e.wait_ge(sv.sem, sv_b + n + 1)
                            dst = graw_st if half == 0 else uraw_st
                            pc = dg if n % 2 == 0 else dx
                            e.dma_start(out=dst[it * 128:(it + 1) * 128, tg * 512:(tg + 1) * 512],
                                        in_=scE1[:, (n % 2) * 512:(n % 2 + 1) * 512]).then_inc(pc.sem, 16)
                            pc.inc(16)
                e.wait_ge(dg.sem, dg.v)
                e.wait_ge(dx.sem, dx.v)

        with nc.Block() as b:
            @b.sync
            def _(e):
                e.dma_start(out=fap(sb_srow, 0, [[192, 1], [1, 128]]),
                            in_=fap(sb_small, SLOT["g_amax"], [[128, 128], [1, 1]])).then_inc(ds.sem, 16)
                e.wait_ge(ds.sem, ds.inc(16))
        with nc.Block() as b:
            @b.vector
            def _(e):
                e.reduce_max(fap(sb_srow, 140, [[192, 1], [1, 1]]), fap(sb_srow, 0, [[192, 1], [1, 128]]), AX)
                e.drain()
        with nc.Block() as b:
            @b.sync
            def _(e):
                e.dma_start(out=fap(sb_srow, 0, [[192, 1], [1, 128]]),
                            in_=fap(sb_small, SLOT["u_amax"], [[128, 128], [1, 1]])).then_inc(ds.sem, 16)
                e.wait_ge(ds.sem, ds.inc(16))
        with nc.Block() as b:
            @b.vector
            def _(e):
                e.reduce_max(fap(sb_srow, 141, [[192, 1], [1, 1]]), fap(sb_srow, 0, [[192, 1], [1, 128]]), AX)
                e.drain()
        with nc.Block() as b:
            @b.sync
            def _(e):
                e.dma_start(out=AR_IN[3][:, :], in_=fap(sb_srow, 140, [[192, 1], [1, 2]])).then_inc(ds.sem, 16)
                e.wait_ge(ds.sem, ds.inc(16))
        with nc.Block() as b:
            @b.gpsimd
            def _(e):
                e.collective_compute("AllReduce", ALU.max,
                                     replica_groups=[list(range(8))],
                                     ins=[AR_IN[3][:, :]], outs=[AR_OUT[3][:, :]]).then_inc(cc.sem, 1)
                cc.inc(1)
        CC_AR3 = cc.v
        A_D2.free()
        A_D1b.free()
        if "q1d" in dbg:
            t = mkdbg("q1d", [D, TC], BF16)
            with nc.Block() as b:
                @b.sync
                def _(e):
                    for k in range(8):
                        e.dma_start(out=t[k * 128:(k + 1) * 128, :],
                                    in_=q1[:, k * TC:(k + 1) * TC]).then_inc(ds.sem, 16)
                    e.wait_ge(ds.sem, ds.inc(8 * 16))
        if "kvg" in dbg:
            t = mkdbg("kvg", [4 * D, TC], BF16)
            with nc.Block() as b:
                @b.sync
                def _(e):
                    e.wait_ge(cc.sem, CC_KV)
                    e.dma_start(out=t[:, :], in_=kvk_g[:, :]).then_inc(ds.sem, 16)
                    e.wait_ge(ds.sem, ds.inc(16))
        if "vvg" in dbg:
            t = mkdbg("vvg", [4 * TC, D], BF16)
            with nc.Block() as b:
                @b.sync
                def _(e):
                    e.wait_ge(cc.sem, CC_KV)
                    e.dma_start(out=t[:, :], in_=kvv_g[:, :]).then_inc(ds.sem, 16)
                    e.wait_ge(ds.sem, ds.inc(16))

    if stage >= 4:
        A_E = arena()
        acc = A_E.sbuf_tensor("acc", [128, 16 * 1024], F32)
        A_E.allocate()
        A_E2 = arena()
        ktb = A_E2.sbuf_tensor("ktb", [128, 8192], BF16)
        vtb = A_E2.sbuf_tensor("vtb", [128, 8192], BF16)
        scT1 = A_E2.sbuf_tensor("scT1", [128, 1024], F32)
        scT2 = A_E2.sbuf_tensor("scT2", [128, 1024], F32)
        scT3 = A_E2.sbuf_tensor("scT3", [128, 1024], F32)
        scP = A_E2.sbuf_tensor("scP", [128, 1024], F32)
        sPt = A_E2.sbuf_tensor("sPt", [128, 1024], BF16)
        mlc = A_E2.sbuf_tensor("mlc", [128, 64], F32)
        A_E2.allocate()
        with nc.Block() as b:
            @b.gpsimd
            def _(e):
                e.memset(acc[:, :], 0.0)
                e.memset(mlc[:, 0:16], -3.0e38)
                e.memset(mlc[:, 16:32], 0.0)
            @b.sync
            def _(e):
                e.wait_ge(cc.sem, CC_KV)

        NKT = 8
        with nc.Block() as b:
            st_b, ds_b, sv_b, ss_b, sg_b = st.v, ds.v, sv.v, ss.v, sg.v
            bslice = (0 if True else 0)
            dw_b = dw.v
            @b.sync
            def _(e, st_b=st_b):
                for kt in range(NKT):
                    r, tl = kt // 2, (kt % 2) * 1024
                    if kt >= 1:
                        e.wait_ge(st.sem, st_b + 48 * kt - 3)
                    e.dma_start(out=ktb[:, :],
                                in_=fap(kvk_g.tensor, r * 1024 * TC + tl,
                                        [[TC, 128], [128 * TC, 8], [1, 1024]])).then_inc(ds.sem, 16)
                    ds.inc(16)
                    if kt >= 1:
                        e.wait_ge(st.sem, st_b + 48 * kt + 1)
                    e.dma_start(out=vtb[:, :],
                                in_=fap(kvv_g.tensor, kt * 1024 * D,
                                        [[D, 128], [128 * D, 8], [1, 1024]])).then_inc(dw.sem, 16)
                    dw.inc(16)
            @b.tensor
            def _(e, ds_b=ds_b, dw_b=dw_b, sv_b=sv_b, ss_b=ss_b):
                def qk(m):
                    qt_ = m % 16
                    for half in range(2):
                        for dk in range(8):
                            mm = e.matmul(P[half][:, :],
                                     q1[:, dk * TC + qt_ * 128:dk * TC + (qt_ + 1) * 128],
                                     ktb[:, dk * 1024 + half * 512:dk * 1024 + (half + 1) * 512],
                                     start=(dk == 0), stop=(dk == 7))
                    mm.then_inc(st.sem, 1)
                    st.inc(1)
                e.wait_ge(ds.sem, ds_b + 16)
                qk(0)
                for n in range(NKT * 16):
                    kt, qt = divmod(n, 16)
                    e.wait_ge(ss.sem, ss_b + 2 * n + 2)
                    if n >= 1:
                        e.wait_ge(sv.sem, sv_b + 5 * (n - 1) + 3)
                    for j in range(8):
                        tr = e.transpose(P[2 + j // 4][0:128, 128 * (j % 4):128 * (j % 4) + 128],
                                         scP[:, j * 128:(j + 1) * 128], sb_idf[:, :])
                    tr.then_inc(st.sem, 1)
                    st.inc(1)
                    if n + 1 < NKT * 16:
                        if (n + 1) % 16 == 0:
                            e.wait_ge(ds.sem, ds_b + ((n + 1) // 16 + 1) * 16)
                        e.wait_ge(sv.sem, sv_b + 5 * n + 1)
                        qk(n + 1)
                    if qt == 0:
                        e.wait_ge(dw.sem, dw_b + (kt + 1) * 16)
                    e.wait_ge(sv.sem, sv_b + 5 * n + 3)
                    for bb in range(8):
                        for dh in range(2):
                            mm = e.matmul(P[4 + dh][:, :],
                                     sPt[:, bb * 128:(bb + 1) * 128],
                                     vtb[:, bb * 1024 + dh * 512:bb * 1024 + (dh + 1) * 512],
                                     start=(bb == 0), stop=(bb == 7))
                    mm.then_inc(st.sem, 1)
                    st.inc(1)
            @b.scalar
            def _(e, st_b=st_b, sv_b=sv_b, sg_b=sg_b):
                for kt in range(NKT):
                    for qt in range(16):
                        n = kt * 16 + qt
                        e.wait_ge(st.sem, st_b + (1 if n == 0 else 3 * n))
                        if n >= 1:
                            e.wait_ge(sv.sem, sv_b + 5 * (n - 1) + 1)
                        e.activation(scT1[:, 0:512], P[0][:, :], AF.Sigmoid)
                        e.activation(scT1[:, 512:1024], P[1][:, :], AF.Sigmoid)
                        e.activation(scT2[:, 0:512], P[0][:, :], AF.Tanh)
                        act = e.activation(scT2[:, 512:1024], P[1][:, :], AF.Tanh)
                        act.then_inc(ss.sem, 1)
                        ss.inc(1)
                        e.wait_ge(sv.sem, sv_b + 5 * n + 2)
                        if n >= 1:
                            e.wait_ge(st.sem, st_b + 3 * (n - 1) + 2)
                        e.activation(mlc[:, 35:36], mlc[:, 34:35], AF.Exp, scale=0.15)
                        e.mul(mlc[:, 36:37], mlc[:, 33:34], -0.15)
                        e.drain()
                        act = e.activation(scP[:, :], scT3[:, :], AF.Exp,
                                           bias=mlc[:, 36:37], scale=0.15,
                                           accum_out=mlc[:, 37:38])
                        act.then_inc(ss.sem, 1)
                        ss.inc(1)
            @b.vector
            def _(e, st_b=st_b, ss_b=ss_b, sg_b=sg_b):
                for kt in range(NKT):
                    for qt in range(16):
                        n = kt * 16 + qt
                        e.wait_ge(st.sem, st_b + (1 if n == 0 else 3 * n))
                        if n >= 1:
                            e.wait_ge(ss.sem, ss_b + 2 * (n - 1) + 2)
                        e.tensor_scalar(scT3[:, 0:512], P[0][:, :], 0.0, 2.0 / 3.0, ALU.max, ALU.mult)
                        e.tensor_scalar(scT3[:, 512:1024], P[1][:, :], 0.0, 2.0 / 3.0,
                                        ALU.max, ALU.mult)
                        e.wait_ge(ss.sem, ss_b + 2 * n + 1)
                        e.scalar_tensor_tensor(scT2[:, :], scT1[:, :], 4.0 / 3.0, scT2[:, :],
                                               ALU.mult, ALU.add)
                        e.drain()
                        e.tensor_tensor(scT3[:, :], scT3[:, :], scT2[:, :], ALU.add)
                        e.drain()
                        e.nop().then_inc(sv.sem, 1)
                        sv.inc(1)
                        e.reduce_max(mlc[:, 32:33], scT3[:, :], AX)
                        e.drain()
                        e.tensor_tensor(mlc[:, 33:34], mlc[:, qt:qt + 1], mlc[:, 32:33], ALU.max)
                        e.drain()
                        e.tensor_tensor(mlc[:, 34:35], mlc[:, qt:qt + 1], mlc[:, 33:34], ALU.subtract)
                        e.tensor_copy(mlc[:, qt:qt + 1], mlc[:, 33:34])
                        e.drain()
                        e.nop().then_inc(sv.sem, 1)
                        sv.inc(1)
                        e.wait_ge(st.sem, st_b + 3 * n + 2)
                        e.tensor_copy(sPt[:, 0:512], P[2][:, :])
                        e.tensor_copy(sPt[:, 512:1024], P[3][:, :]).then_inc(sv.sem, 1)
                        sv.inc(1)
                        e.wait_ge(ss.sem, ss_b + 2 * n + 2)
                        e.scalar_tensor_tensor(mlc[:, 16 + qt:17 + qt], mlc[:, 16 + qt:17 + qt],
                                               mlc[:, 35:36], mlc[:, 37:38], ALU.mult, ALU.add)
                        e.drain()
                        e.nop().then_inc(sv.sem, 1)
                        sv.inc(1)
                        e.wait_ge(st.sem, st_b + (3 * n + 4 if n < NKT * 16 - 1 else 3 * n + 3))
                        for dh in range(2):
                            e.scalar_tensor_tensor(acc[:, qt * 1024 + dh * 512:qt * 1024 + (dh + 1) * 512],
                                                   acc[:, qt * 1024 + dh * 512:qt * 1024 + (dh + 1) * 512],
                                                   mlc[:, 35:36], P[4 + dh][:, :],
                                                   ALU.mult, ALU.add)
                        e.drain()
                        e.nop().then_inc(sv.sem, 1)
                        sv.inc(1)

        with nc.Block() as b:
            @b.vector
            def _(e):
                for qt in range(16):
                    e.reciprocal(mlc[:, 32:33], mlc[:, 16 + qt:17 + qt])
                    e.drain()
                    e.tensor_scalar(acc[:, qt * 1024:(qt + 1) * 1024],
                                    acc[:, qt * 1024:(qt + 1) * 1024],
                                    mlc[:, 32:33], None, ALU.mult)
                    e.drain()
        if "acc" in dbg:
            t = mkdbg("acc", [TC, D])
            with nc.Block() as b:
                @b.sync
                def _(e):
                    for k in range(16):
                        e.dma_start(out=t[k * 128:(k + 1) * 128, :],
                                    in_=acc[:, k * 1024:(k + 1) * 1024]).then_inc(ds.sem, 16)
                    e.wait_ge(ds.sem, ds.inc(16 * 16))

    if stage >= 5:
        A_E2.free()
        A_F = arena()
        scY1 = A_F.sbuf_tensor("scY1", [128, 1024], BF16)
        wF = A_F.sbuf_tensor("wF", [128, 8 * 1024], BF16)
        stg1 = A_F.sbuf_tensor("stg1", [128, 1024], F32)
        A_F.allocate()
        with nc.Block() as b:
            @b.gpsimd
            def _(e):
                for kk in range(8):
                    e.dma_start(out=wF[:, kk * 1024:(kk + 1) * 1024],
                                in_=Wo_t[kk * 128:(kk + 1) * 128, :]).then_inc(dg.sem, 16)
                    dg.inc(16)
                e.wait_ge(dg.sem, dg.v)
        for t16 in range(16):
            with nc.Block() as b:
                @b.tensor
                def _(e, t16=t16):
                    for j in range(8):
                        e.transpose(P[j // 4][0:128, 128 * (j % 4):128 * (j % 4) + 128],
                                    acc[:, t16 * 1024 + j * 128:t16 * 1024 + (j + 1) * 128],
                                    sb_idf[:, :])
            with nc.Block() as b:
                @b.vector
                def _(e, t16=t16):
                    e.tensor_copy(scY1[:, 0:512], P[0][:, :])
                    e.tensor_copy(scY1[:, 512:1024], P[1][:, :])
            for og in range(2):
                with nc.Block() as b:
                    @b.tensor
                    def _(e, t16=t16, og=og):
                        for o4 in range(4):
                            o = og * 4 + o4
                            for hf in range(8):
                                e.matmul(P[4 + o4][:, 0:128],
                                         wF[:, hf * 1024 + o * 128:hf * 1024 + (o + 1) * 128],
                                         scY1[:, hf * 128:(hf + 1) * 128],
                                         start=(hf == 0), stop=(hf == 7))
                with nc.Block() as b:
                    @b.vector
                    def _(e, t16=t16, og=og):
                        for o4 in range(4):
                            o = og * 4 + o4
                            e.tensor_scalar(stg1[:, o * 128:(o + 1) * 128], P[4 + o4][:, 0:128],
                                            sb_small[:, 96 + o:96 + o + 1], None, ALU.add)
            with nc.Block() as b:
                @b.sync
                def _(e, t16=t16):
                    e.dma_start(out=fap(attn0_st.tensor, 128 * t16,
                                        [[TC, 128], [128 * TC, 8], [1, 128]]),
                                in_=fap(stg1, 0, [[1024, 128], [128, 8], [1, 128]])).then_inc(ds.sem, 16)
                    e.wait_ge(ds.sem, ds.inc(16))
        if "attn1" in dbg:
            t = mkdbg("attn1", [D, TC])
            with nc.Block() as b:
                @b.sync
                def _(e):
                    e.dma_start(out=t[:, :], in_=attn0_st[:, :]).then_inc(ds.sem, 16)
                    e.wait_ge(ds.sem, ds.inc(16))
        A_F.free()
        A_E.free()

        A_F2 = arena()
        sgA = A_F2.sbuf_tensor("sgA", [128, 12 * 512], F32)
        sgB = A_F2.sbuf_tensor("sgB", [128, 12 * 512], F32)
        sgC = A_F2.sbuf_tensor("sgC", [128, 12 * 512], F32)
        wti2 = A_F2.sbuf_tensor("wti2", [128, 2 * 1024], BF16)
        scQ2 = A_F2.sbuf_tensor("scQ2", [128, 2 * 512], BF16)
        scX2 = A_F2.sbuf_tensor("scX2", [128, 8 * 512], F32)
        A_F2.allocate()

        def scalepair(ar_idx, names):
            with nc.Block() as b:
                @b.sync
                def _(e):
                    e.wait_ge(cc.sem, CC_TRACK[ar_idx])
                    e.dma_start(out=fap(sb_srow, 144, [[192, 1], [1, 2]]),
                                in_=AR_OUT[ar_idx][:, :]).then_inc(ds.sem, 16)
                    e.wait_ge(ds.sem, ds.inc(16))
            with nc.Block() as b:
                @b.vector
                def _(e):
                    for i2, nm in enumerate(names):
                        if nm is None:
                            continue
                        r_in = fap(sb_srow, 144 + i2, [[192, 1], [1, 1]])
                        r_s = fap(sb_srow, 146 + 2 * i2, [[192, 1], [1, 1]])
                        r_rs = fap(sb_srow, 147 + 2 * i2, [[192, 1], [1, 1]])
                        e.tensor_scalar(r_s, r_in, 1.0 / 32767.0, 1e-12, ALU.mult, ALU.max)
                        e.drain()
                        e.reciprocal(r_rs, r_s)
                        e.drain()
            with nc.Block() as b:
                @b.sync
                def _(e):
                    e.dma_start(out=stats_dram[0:1, 0:6],
                                in_=fap(sb_srow, 144, [[192, 1], [1, 6]])).then_inc(ds.sem, 16)
                    e.wait_ge(ds.sem, ds.inc(16))
                    for i2, nm in enumerate(names):
                        if nm is None:
                            continue
                        e.dma_start(out=slot("s_" + nm),
                                    in_=fap(stats_dram.tensor, 2 + 2 * i2, [[0, 128], [1, 1]])).then_inc(ds.sem, 16)
                        e.dma_start(out=slot("rs_" + nm),
                                    in_=fap(stats_dram.tensor, 3 + 2 * i2, [[0, 128], [1, 1]])).then_inc(ds.sem, 16)
                        e.wait_ge(ds.sem, ds.inc(32))

        CC_TRACK = {3: CC_AR3}
        scalepair(3, ("g1", "u1"))
        with nc.Block() as b:
            @b.vector
            def _(e):
                e.tensor_scalar(slot("g2f"), slot("s_g1"), 1.0 / 65536.0, None, ALU.mult)
                e.drain()

        amax2 = slot("g2_amax")
        for sb8 in range(NT1 // 3):
            tiles = [(sb8 * 3 + jj // 4, jj % 4) for jj in range(12)]
            with nc.Block() as b:
                @b.gpsimd
                def _(e, tiles=tiles):
                    for jj, (it, tg) in enumerate(tiles):
                        e.dma_start(out=sgA[:, jj * 512:(jj + 1) * 512],
                                    in_=graw_st[it * 128:(it + 1) * 128, tg * 512:(tg + 1) * 512]).then_inc(dg.sem, 16)
                        dg.inc(16)
                    e.wait_ge(dg.sem, dg.v)
            with nc.Block() as b:
                @b.vector
                def _(e):
                    for k in range(6):
                        e.tensor_scalar(sgB[:, k * 1024:(k + 1) * 1024], sgA[:, k * 1024:(k + 1) * 1024],
                                        slot("rs_g1"), MAGIC, ALU.mult, ALU.add)
                    e.drain()
                    for k in range(6):
                        e.tensor_scalar(sgB[:, k * 1024:(k + 1) * 1024], sgB[:, k * 1024:(k + 1) * 1024],
                                        -MAGIC, None, ALU.add)
                    e.drain()
            with nc.Block() as b:
                @b.scalar
                def _(e):
                    for k in range(6):
                        e.activation(sgC[:, k * 1024:(k + 1) * 1024], sgB[:, k * 1024:(k + 1) * 1024],
                                     AF.Sigmoid, scale=slot("s_g1"))
            with nc.Block() as b:
                @b.vector
                def _(e, sb8=sb8):
                    for k in range(6):
                        e.tensor_scalar(sgC[:, k * 1024:(k + 1) * 1024], sgC[:, k * 1024:(k + 1) * 1024],
                                        65536.0, MAGIC, ALU.mult, ALU.add)
                    e.drain()
                    for k in range(6):
                        e.tensor_scalar(sgC[:, k * 1024:(k + 1) * 1024], sgC[:, k * 1024:(k + 1) * 1024],
                                        -MAGIC, 65535.0, ALU.add, ALU.min)
                    e.drain()
                    for k in range(6):
                        e.tensor_tensor(sgA[:, k * 1024:(k + 1) * 1024], sgB[:, k * 1024:(k + 1) * 1024],
                                        sgC[:, k * 1024:(k + 1) * 1024], ALU.mult)
                    e.drain()
                    for k in range(6):
                        e.tensor_scalar(sgA[:, k * 1024:(k + 1) * 1024], sgA[:, k * 1024:(k + 1) * 1024],
                                        slot("g2f"), None, ALU.mult)
                    e.drain()
                    e.reduce_max(slot("rawm"), sgA[:, :], AX, apply_absolute_value=True)
                    e.drain()
                    if sb8 == 0:
                        e.tensor_copy(amax2, slot("rawm"))
                    else:
                        e.tensor_tensor(amax2, amax2, slot("rawm"), ALU.max)
                    e.drain()
            with nc.Block() as b:
                @b.gpsimd
                def _(e, tiles=tiles):
                    for jj, (it, tg) in enumerate(tiles):
                        e.dma_start(out=g2_st[it * 128:(it + 1) * 128, tg * 512:(tg + 1) * 512],
                                    in_=sgA[:, jj * 512:(jj + 1) * 512]).then_inc(dg.sem, 16)
                        dg.inc(16)
                    e.wait_ge(dg.sem, dg.v)
        with nc.Block() as b:
            @b.sync
            def _(e):
                e.dma_start(out=fap(sb_srow, 0, [[192, 1], [1, 128]]),
                            in_=fap(sb_small, SLOT["g2_amax"], [[128, 128], [1, 1]])).then_inc(ds.sem, 16)
                e.wait_ge(ds.sem, ds.inc(16))
        with nc.Block() as b:
            @b.vector
            def _(e):
                e.reduce_max(fap(sb_srow, 140, [[192, 1], [1, 1]]), fap(sb_srow, 0, [[192, 1], [1, 128]]), AX)
                e.drain()
                e.tensor_copy(fap(sb_srow, 141, [[192, 1], [1, 1]]), fap(sb_srow, 140, [[192, 1], [1, 1]]))
                e.drain()
        with nc.Block() as b:
            @b.sync
            def _(e):
                e.dma_start(out=AR_IN[4][:, :], in_=fap(sb_srow, 140, [[192, 1], [1, 2]])).then_inc(ds.sem, 16)
                e.wait_ge(ds.sem, ds.inc(16))
        with nc.Block() as b:
            @b.gpsimd
            def _(e):
                e.collective_compute("AllReduce", ALU.max,
                                     replica_groups=[list(range(8))],
                                     ins=[AR_IN[4][:, :]], outs=[AR_OUT[4][:, :]]).then_inc(cc.sem, 1)
                cc.inc(1)
        CC_TRACK[4] = cc.v
        scalepair(4, ("g2", None))
        with nc.Block() as b:
            @b.vector
            def _(e):
                e.tensor_tensor(slot("ouf"), slot("s_g2"), slot("s_u1"), ALU.mult)
                e.drain()

        amax3 = slot("o_amax")
        for sb8 in range(NT1 // 3):
            tiles = [(sb8 * 3 + jj // 4, jj % 4) for jj in range(12)]
            with nc.Block() as b:
                @b.gpsimd
                def _(e, tiles=tiles):
                    for jj, (it, tg) in enumerate(tiles):
                        e.dma_start(out=sgA[:, jj * 512:(jj + 1) * 512],
                                    in_=g2_st[it * 128:(it + 1) * 128, tg * 512:(tg + 1) * 512]).then_inc(dg.sem, 16)
                        dg.inc(16)
                        e.dma_start(out=sgC[:, jj * 512:(jj + 1) * 512],
                                    in_=uraw_st[it * 128:(it + 1) * 128, tg * 512:(tg + 1) * 512]).then_inc(dg.sem, 16)
                        dg.inc(16)
                    e.wait_ge(dg.sem, dg.v)
            with nc.Block() as b:
                @b.vector
                def _(e, sb8=sb8):
                    for k in range(6):
                        e.tensor_scalar(sgB[:, k * 1024:(k + 1) * 1024], sgA[:, k * 1024:(k + 1) * 1024],
                                        slot("rs_g2"), MAGIC, ALU.mult, ALU.add)
                        e.tensor_scalar(sgC[:, k * 1024:(k + 1) * 1024], sgC[:, k * 1024:(k + 1) * 1024],
                                        slot("rs_u1"), MAGIC, ALU.mult, ALU.add)
                    e.drain()
                    for k in range(6):
                        e.tensor_scalar(sgB[:, k * 1024:(k + 1) * 1024], sgB[:, k * 1024:(k + 1) * 1024],
                                        -MAGIC, None, ALU.add)
                        e.tensor_scalar(sgC[:, k * 1024:(k + 1) * 1024], sgC[:, k * 1024:(k + 1) * 1024],
                                        -MAGIC, None, ALU.add)
                    e.drain()
                    for k in range(6):
                        e.tensor_tensor(sgA[:, k * 1024:(k + 1) * 1024], sgB[:, k * 1024:(k + 1) * 1024],
                                        sgC[:, k * 1024:(k + 1) * 1024], ALU.mult)
                    e.drain()
                    for k in range(6):
                        e.tensor_scalar(sgA[:, k * 1024:(k + 1) * 1024], sgA[:, k * 1024:(k + 1) * 1024],
                                        slot("ouf"), None, ALU.mult)
                    e.drain()
                    e.reduce_max(slot("rawm"), sgA[:, :], AX, apply_absolute_value=True)
                    e.drain()
                    if sb8 == 0:
                        e.tensor_copy(amax3, slot("rawm"))
                    else:
                        e.tensor_tensor(amax3, amax3, slot("rawm"), ALU.max)
                    e.drain()
            with nc.Block() as b:
                @b.gpsimd
                def _(e, tiles=tiles):
                    for jj, (it, tg) in enumerate(tiles):
                        e.dma_start(out=oo_st[it * 128:(it + 1) * 128, tg * 512:(tg + 1) * 512],
                                    in_=sgA[:, jj * 512:(jj + 1) * 512]).then_inc(dg.sem, 16)
                        dg.inc(16)
                    e.wait_ge(dg.sem, dg.v)
        with nc.Block() as b:
            @b.sync
            def _(e):
                e.dma_start(out=fap(sb_srow, 0, [[192, 1], [1, 128]]),
                            in_=fap(sb_small, SLOT["o_amax"], [[128, 128], [1, 1]])).then_inc(ds.sem, 16)
                e.wait_ge(ds.sem, ds.inc(16))
        with nc.Block() as b:
            @b.vector
            def _(e):
                e.reduce_max(fap(sb_srow, 140, [[192, 1], [1, 1]]), fap(sb_srow, 0, [[192, 1], [1, 128]]), AX)
                e.drain()
                e.tensor_copy(fap(sb_srow, 141, [[192, 1], [1, 1]]), fap(sb_srow, 140, [[192, 1], [1, 1]]))
                e.drain()
        with nc.Block() as b:
            @b.sync
            def _(e):
                e.dma_start(out=AR_IN[5][:, :], in_=fap(sb_srow, 140, [[192, 1], [1, 2]])).then_inc(ds.sem, 16)
                e.wait_ge(ds.sem, ds.inc(16))
        with nc.Block() as b:
            @b.gpsimd
            def _(e):
                e.collective_compute("AllReduce", ALU.max,
                                     replica_groups=[list(range(8))],
                                     ins=[AR_IN[5][:, :]], outs=[AR_OUT[5][:, :]]).then_inc(cc.sem, 1)
                cc.inc(1)
        CC_TRACK[5] = cc.v
        scalepair(5, ("o1", None))

        for tg in range(4):
            with nc.Block() as b:
                st_b, ds_b, sv_b, dg_b, dw_b, dy_b = st.v, ds.v, sv.v, dg.v, dw.v, dy.v
                @b.sync
                def _(e, tg=tg, st_b=st_b):
                    for it in range(NT1):
                        if it >= 2:
                            e.wait_ge(st.sem, st_b + it - 1)
                        pc = ds if it % 2 == 0 else dw
                        e.dma_start(
                            out=wti2[:, (it % 2) * 1024:(it % 2 + 1) * 1024],
                            in_=fap(dwn1.tensor, it * 128 * 1024,
                                    [[1024, 128], [1, 1024]])).then_inc(pc.sem, 16)
                        pc.inc(16)
                @b.gpsimd
                def _(e, tg=tg, st_b=st_b):
                    for it in range(NT1):
                        if it >= 2:
                            e.wait_ge(st.sem, st_b + it - 1)
                        pc = dg if it % 2 == 0 else dy
                        e.dma_start(out=sgA[:, (it % 2) * 512:(it % 2 + 1) * 512],
                                    in_=oo_st[it * 128:(it + 1) * 128, tg * 512:(tg + 1) * 512]).then_inc(pc.sem, 16)
                        pc.inc(16)
                @b.vector
                def _(e, tg=tg, st_b=st_b, ds_b=ds_b, dg_b=dg_b, dw_b=dw_b, dy_b=dy_b):
                    for it in range(NT1):
                        pb, pv = (ds_b, ds) if it % 2 == 0 else (dw_b, dw)
                        e.wait_ge(pv.sem, pb + (it // 2 + 1) * 16)
                        pb2, pv2 = (dg_b, dg) if it % 2 == 0 else (dy_b, dy)
                        e.wait_ge(pv2.sem, pb2 + (it // 2 + 1) * 16)
                        if it >= 2:
                            e.wait_ge(st.sem, st_b + it - 1)
                        e.tensor_scalar(sgB[:, 0:512], sgA[:, (it % 2) * 512:(it % 2 + 1) * 512],
                                        slot("rs_o1"), MAGIC, ALU.mult, ALU.add)
                        e.drain()
                        e.tensor_scalar(scQ2[:, (it % 2) * 512:(it % 2 + 1) * 512],
                                        sgB[:, 0:512], -MAGIC, None, ALU.add).then_inc(sv.sem, 1)
                        sv.inc(1)
                @b.tensor
                def _(e, tg=tg, ds_b=ds_b, sv_b=sv_b):
                    for it in range(NT1):
                        e.wait_ge(sv.sem, sv_b + it + 1)
                        for o in range(8):
                            mm = e.matmul(P[o][:, :],
                                     wti2[:, (it % 2) * 1024 + o * 128:(it % 2) * 1024 + (o + 1) * 128],
                                     scQ2[:, (it % 2) * 512:(it % 2 + 1) * 512],
                                     start=(it == 0), stop=(it == NT1 - 1))
                        mm.then_inc(st.sem, 1)
                        st.inc(1)
            with nc.Block() as b:
                ds_b, sv_b, dg_b, dx_b = ds.v, sv.v, dg.v, dx.v
                @b.sync
                def _(e, tg=tg, sv_b=sv_b):
                    for o in range(8):
                        e.dma_start(out=scX2[:, o * 512:(o + 1) * 512],
                                    in_=x1_st[o * 128:(o + 1) * 128, tg * 512:(tg + 1) * 512]).then_inc(ds.sem, 16)
                        ds.inc(16)
                    for o in range(8):
                        pc = dg if o % 2 == 0 else dx
                        if o >= 2:
                            e.wait_ge(sv.sem, sv_b + o - 1)
                        e.dma_start(out=sgA[:, (o % 2) * 512:(o % 2 + 1) * 512],
                                    in_=attn0_st[o * 128:(o + 1) * 128, tg * 512:(tg + 1) * 512]).then_inc(pc.sem, 16)
                        pc.inc(16)
                @b.vector
                def _(e, tg=tg, ds_b=ds_b, dg_b=dg_b, dx_b=dx_b):
                    for o in range(8):
                        if o == 0:
                            e.wait_ge(ds.sem, ds_b + 8 * 16)
                        pb, pv = (dg_b, dg) if o % 2 == 0 else (dx_b, dx)
                        e.wait_ge(pv.sem, pb + (o // 2 + 1) * 16)
                        e.tensor_tensor(scX2[:, o * 512:(o + 1) * 512],
                                        scX2[:, o * 512:(o + 1) * 512],
                                        sgA[:, (o % 2) * 512:(o % 2 + 1) * 512], ALU.add)
                        e.drain()
                        e.scalar_tensor_tensor(scX2[:, o * 512:(o + 1) * 512],
                                               P[o][:, :], slot("s_o1"),
                                               scX2[:, o * 512:(o + 1) * 512],
                                               ALU.mult, ALU.add)
                        e.drain()
                        e.tensor_copy(slot("asmtk"), ones_col).then_inc(sv.sem, 1)
                        sv.inc(1)
                @b.scalar
                def _(e, tg=tg, sv_b=sv_b):
                    for o in range(8):
                        e.wait_ge(sv.sem, sv_b + o + 1)
                        e.dma_start(out=out_fm[o * 128:(o + 1) * 128, tg * 512:(tg + 1) * 512],
                                    in_=scX2[:, o * 512:(o + 1) * 512]).then_inc(da.sem, 16)
                        da.inc(16)
                    e.wait_ge(da.sem, da.v)
        for nm, src, rows in (("graw", graw_st, I1), ("uraw", uraw_st, I1),
                              ("g2s", g2_st, I1), ("oos", oo_st, I1)):
            if nm in dbg:
                t = mkdbg(nm, [rows, TC], F16)
                with nc.Block() as b:
                    @b.sync
                    def _(e, t=t, src=src, rows=rows):
                        e.dma_start(out=t[:, :], in_=src[0:rows, :]).then_inc(ds.sem, 16)
                        e.wait_ge(ds.sem, ds.inc(16))
        A_F2.free()

    if "wbg" in dbg:
        t = mkdbg("wbg", [WB_ROWS, 128], BF16)
        with nc.Block() as b:
            @b.sync
            def _(e):
                e.wait_ge(cc.sem, 1)
                e.dma_start(out=t[:, :], in_=wb_g[0:WB_ROWS, :]).then_inc(ds.sem, 16)
                e.wait_ge(ds.sem, ds.inc(16))
    with nc.Block() as b:
        @b.sync
        def _(e):
            e.wait_ge(ds.sem, ds.v)
        @b.gpsimd
        def _(e):
            e.wait_ge(dg.sem, dg.v)

    for a in reversed(list(LIVE)):
        a.free()
    ctx.close()
    return nc, DBG


def _layout_inputs(inputs, core):
    I = inputs
    b, j = core // 4, core % 4
    f32 = np.float32
    x = np.asarray(I["L0_attn_x"], f32)
    d = {}
    d["x_fm"] = np.ascontiguousarray(x[b, j * TC:(j + 1) * TC, :].T)
    d["qw_t"] = np.ascontiguousarray(np.asarray(I["L0_attn_q_w"], f32).T)
    d["kw_t"] = np.ascontiguousarray(np.asarray(I["L0_attn_k_w"], f32).T)
    d["vw_t"] = np.ascontiguousarray(np.asarray(I["L0_attn_v_w"], f32).T)
    d["ow_t"] = np.ascontiguousarray(np.asarray(I["L0_attn_o_w"], f32).T)
    d["cs1_t"] = np.ascontiguousarray(np.asarray(I["L0_attn_cs_w1"], f32).T)
    d["cs2_t"] = np.ascontiguousarray(np.asarray(I["L0_attn_cs_w2"], f32).T)
    d["cs_b1"] = np.asarray(I["L0_attn_cs_b1"], f32).reshape(256, 1)
    d["cs_b2"] = np.asarray(I["L0_attn_cs_b2"], f32).reshape(16, 1)
    dec = np.asarray(I["L0_attn_decay_params"], f32).reshape(1024)
    d["decay_he"] = np.ascontiguousarray(dec.reshape(8, 128).T)
    d["clip_x"] = np.asarray(I["L0_ffn_clip_x"], f32).reshape(1, 1)
    d["clip_down"] = np.asarray(I["L0_ffn_clip_down"], f32).reshape(1, 1)

    def slices3(w, core):
        tiles = []
        nt = w.shape[0] // 128
        for sl in range(3):
            ti = core + 8 * sl
            if ti < nt:
                tiles.append(w[ti * 128:(ti + 1) * 128, :])
            else:
                tiles.append(np.ones((128, w.shape[1]), f32))
        return np.ascontiguousarray(np.concatenate(tiles, 0))

    wg0 = np.asarray(I["L0_ffn_w_gate"], f32); wu0 = np.asarray(I["L0_ffn_w_up"], f32)
    wd0 = np.asarray(I["L0_ffn_w_down"], f32)
    d["l0g_own"] = slices3(wg0, core); d["l0u_own"] = slices3(wu0, core)
    d["l0d_own"] = np.ascontiguousarray(wd0[core * 128:(core + 1) * 128, :])
    wg1 = np.asarray(I["L1_ffn_w_gate"], f32); wu1 = np.asarray(I["L1_ffn_w_up"], f32)
    wd1 = np.asarray(I["L1_ffn_w_down"], f32)
    d["l1g_own"] = slices3(wg1, core); d["l1u_own"] = slices3(wu1, core)
    d["l1d_own"] = np.ascontiguousarray(wd1[core * 128:(core + 1) * 128, :])

    d["Wq_t"] = np.ascontiguousarray(np.asarray(I["L1_attn_Wq"], f32).T)
    d["Wk_t"] = np.ascontiguousarray(np.asarray(I["L1_attn_Wk"], f32).T)
    d["Wv_t"] = np.ascontiguousarray(np.asarray(I["L1_attn_Wv"], f32).T)
    d["Wo_t"] = np.ascontiguousarray(np.asarray(I["L1_attn_Wo"], f32).T)
    d["bq"] = np.asarray(I["L1_attn_bq"], f32).reshape(D, 1)
    d["bk"] = np.asarray(I["L1_attn_bk"], f32).reshape(D, 1)
    d["bo"] = np.asarray(I["L1_attn_bo"], f32).reshape(D, 1)
    d["bv_row"] = np.asarray(I["L1_attn_bv"], f32).reshape(1, D)

    d["ident_f"] = np.eye(128, dtype=f32)
    sel = np.zeros((128, 128), np.uint8)
    for s_ in range(128):
        for u in range(128):
            sel[s_, u] = 1 if (s_ % 64) <= (u % 64) else 0
    d["selmask"] = np.ascontiguousarray(np.tile(sel, (1, 4)))
    m32 = np.ones((128, SBL), f32)
    m32[:, ::32] = 0.0
    d["mask32"] = m32
    r = core % 4
    fl = np.array([1.0 if i < r else 0.0 for i in range(4)], f32)
    d["sflags_bc"] = np.broadcast_to(fl, (128, 4)).copy()
    return d


def kernel(**inputs):
    nc, _ = build(stage=9, dbg=())
    in_maps = [_layout_inputs(inputs, c) for c in range(8)]
    res = run_bass_kernel_spmd(nc, in_maps, core_ids=list(range(8)))
    out = np.empty((2, 8192, D), np.float32)
    for c in range(8):
        b, j = c // 4, c % 4
        out[b, j * TC:(j + 1) * TC, :] = np.asarray(res.results[c]["out_fm"]).T
    return out
